# version 14
# baseline (speedup 1.0000x reference)
import numpy as np
import ml_dtypes

import concourse.bass as bass
import concourse.mybir as mybir
from concourse import dve_ops
from concourse.dve_spec import Spec, Src0, Src1, One, C2, lower
from concourse.dve_table_gen import dve_ver_for
from concourse.bass_utils import run_bass_kernel_spmd

F32 = mybir.dt.float32
BF16 = mybir.dt.bfloat16
ALU = mybir.AluOpType
AX = mybir.AxisListType
AF = mybir.ActivationFunctionType


def _register_qdq_ops():
    have = {o.name: o for o in dve_ops.OPS}
    if "QDQ_A" in have:
        return have["QDQ_A"], have["QDQ_T"]
    from concourse.dve_spec import C0, C1
    u = Src0
    a_body = ((u > One) + (u > C0)) + ((u > C1) + (u > C2))
    c5, c6, c7 = u > C0, u > C1, u > C2
    i1 = c5 + c6
    i2 = c7 + c7
    j = i1 + i2
    t_body = Src1 + (j + j)

    def a_ref(in0, in1, s0, s1, imm2):
        x = in0
        one = np.asarray(1.0, x.dtype)
        zero = np.asarray(0.0, x.dtype)
        r = np.where(x > np.asarray(1.0, x.dtype), one, zero)
        for K in (s0, s1, imm2):
            r = r + np.where(x > np.asarray(K, x.dtype), one, zero)
        return r.astype(x.dtype)

    def t_ref(in0, in1, s0, s1, imm2):
        uu, A = in0, in1
        one = np.asarray(1.0, uu.dtype)
        zero = np.asarray(0.0, uu.dtype)
        j = np.where(uu > np.asarray(s0, uu.dtype), one, zero)
        j = j + np.where(uu > np.asarray(s1, uu.dtype), one, zero)
        j = j + np.asarray(2.0, uu.dtype) * np.where(
            uu > np.asarray(imm2, uu.dtype), one, zero)
        return (A + j + j).astype(uu.dtype)

    ver = dve_ver_for("TRN2")
    spec_a, spec_t = Spec(body=a_body), Spec(body=t_body)
    op_a = dve_ops.DveOp("QDQ_A", spec_a, subdim=False,
                         uops_sha={ver: lower(spec_a, ver=ver, name="QDQ_A").sha(ver)},
                         reference=a_ref)
    op_t = dve_ops.DveOp("QDQ_T", spec_t, subdim=False,
                         uops_sha={ver: lower(spec_t, ver=ver, name="QDQ_T").sha(ver)},
                         reference=t_ref)
    names = [o.name for o in dve_ops.OPS]
    for op, old in ((op_a, "DVEHASH32L_SEED"), (op_t, "DVEHASH32L_BODY")):
        idx = names.index(old)
        dve_ops.OPS[idx] = op
        row = dve_ops._SUB_OPCODE_FOR_NAME.pop(old)
        dve_ops._SUB_OPCODE_FOR_NAME[op.name] = row
        dve_ops.CUSTOM_DVE_SPECS.pop(old, None)
        dve_ops.CUSTOM_DVE_SPECS[op.name] = op.spec
        dve_ops.CUSTOM_DVE_REFS.pop(old, None)
        dve_ops.CUSTOM_DVE_REFS[op.name] = op.ref
    return op_a, op_t


QDQ_A, QDQ_T = _register_qdq_ops()

D = 1024
I = 2816
NH = 16
HD = 64
F2 = 128
FF = 256
NCORE = 8
NGRP = 4
ISH = I // 8
WPART = 8 * ISH
WRANK = 3 * WPART
EPS_LN = 1e-5
EPS_Z = 256e-6
ARENA = 101120

X0 = 0
LNR = 32768
XQR = 36864
QDQ_SC = 38912
QSM = 47872
SILU = 52096
HR = 53504
HQR = 56320
XQT = 61952
HQT = 66048
WBUF = 77312
WDB = 88576
OUTR = 90624
WSH = 92672
HR2 = 92672
HQ23 = 95488
WSHQ = 66048
WSHT = 77312
PML = 88576
A_LN2B = 32768
A_K = 0
A_RANK = 0
A_KVACC = 4224
A_ATT = 0
A_LN2T = 16384
A_QF = 16384
A_QFT = 24576
A_Q = 32768
A_RZT = 65792
RZF = 65792
EARG = 66816
A_XT = 37888
A_V65 = 49152
A_KVGB = 49152
A_KVT2 = 53248
A_Q2 = 57344
A_KVP = 65792
A_WV = 74240
A_LN2R = 82432
A_KF = 86528
A_WQK = 87040


def _epilogue(block, wt, cnt):
    @block.sync
    def _(s):
        for ch in ("sLDX", "sLDPM", "sLDSH", "sSP", "sAGW", "sAGK",
                   "sRK", "sST", "sX1", "sXS20", "sXS21", "sOUT",
                   "sWG0", "sWG1", "sWD0", "sWD1", "sWD2", "sWD3",
                   "sOR0", "sOR1", "sX2P0", "sX2P1"):
            wt(s, ch, cnt[ch])

    @block.scalar
    def _(sc):
        for ch in ("sTPX0", "sTPX1", "sTPH0", "sTPH1", "sTPL", "sTPQ",
                   "sTPK", "sTPW"):
            wt(sc, ch, cnt[ch])
        wt(sc, "cS", cnt["cS"])

    @block.gpsimd
    def _(g):
        for ch in ("sQK0", "sQK1", "sWV", "cc1", "cc2"):
            wt(g, ch, cnt[ch])
        wt(g, "cG", cnt["cG"])

    @block.tensor
    def _(tt):
        wt(tt, "cT", cnt["cT"])

    @block.vector
    def _(v):
        wt(v, "cV", cnt["cV"])


def build_nc(NT=16, stop_after=None, dbg_spill=False, dbg_attn=False):
    TOK = NT * 128
    NTQ = max(NT // 4, 1)
    NQ = NT // NTQ
    TB = TOK // 512
    nc = bass.Bass("TRN2", target_bir_lowering=False)

    x_d = nc.dram_tensor("x", [TOK, D], F32, kind="ExternalInput").ap()
    wqkv_d = nc.dram_tensor("wqkv", [D, 3 * D], F32, kind="ExternalInput").ap()
    wout_d = nc.dram_tensor("wout", [D, D], F32, kind="ExternalInput").ap()
    pmt_d = nc.dram_tensor("pmt", [HD, NH * F2], F32, kind="ExternalInput").ap()
    wg_d = nc.dram_tensor("wg", [ISH, D], F32, kind="ExternalInput").ap()
    wu_d = nc.dram_tensor("wu", [ISH, D], F32, kind="ExternalInput").ap()
    wd_d = nc.dram_tensor("wd", [128, I], F32, kind="ExternalInput").ap()
    out_d = nc.dram_tensor("out", [TOK, D], F32, kind="ExternalOutput").ap()

    agw_in = nc.dram_tensor("agw_in", [128, WRANK], BF16)
    agw_out = nc.dram_tensor("agw_out", [NCORE, 128, WRANK], BF16)
    wdc = nc.dram_tensor("wdc", [2 * 22 * 128 * 512], BF16)
    KVE = 65 * NH * 264
    agk_in = nc.dram_tensor("agk_in", [1, KVE + 32], BF16)
    agk_out = nc.dram_tensor("agk_out", [NGRP, KVE + 32], BF16)
    spk = dict(kind="ExternalOutput") if dbg_spill else {}
    xsp1 = nc.dram_tensor("xsp1", [TOK, D], F32, **spk)
    xsp2 = nc.dram_tensor("xsp2", [TOK, D], F32, **spk)
    if dbg_attn:
        dbga = nc.dram_tensor("dbga", [128, 8 * TOK], BF16,
                              kind="ExternalOutput").ap()
        dbgr = nc.dram_tensor("dbgr", [128, 512], BF16,
                              kind="ExternalOutput").ap()
        dbgq = nc.dram_tensor("dbgq", [128, 4 * TOK], BF16,
                              kind="ExternalOutput").ap()
        dbgk = nc.dram_tensor("dbgk", [128, 4096], BF16,
                              kind="ExternalOutput").ap()
        dbgc = nc.dram_tensor("dbgc", [128, 4], F32,
                              kind="ExternalOutput").ap()
        dbgs = nc.dram_tensor("dbgs", [128, 32], BF16,
                              kind="ExternalOutput").ap()
        dbgz = nc.dram_tensor("dbgz", [128, 1024], F32,
                              kind="ExternalOutput").ap()

    ctx = {}
    from contextlib import ExitStack
    es = ExitStack()
    with es:
        ARN = es.enter_context(nc.sbuf_tensor("arena", [128, ARENA], BF16))
        pm2 = es.enter_context(nc.sbuf_tensor("pm2", [128, 8 * FF], BF16))
        identb = es.enter_context(nc.sbuf_tensor("identb", [128, 128], BF16))
        identf = es.enter_context(nc.sbuf_tensor("identf", [128, 128], F32))
        onesf = es.enter_context(nc.sbuf_tensor("onesf", [1, 130], F32))
        epsrow = es.enter_context(nc.sbuf_tensor("epsrow", [1, 512], F32))
        nhalf = es.enter_context(nc.sbuf_tensor("nhalf", [128, 2], BF16))
        negstab = es.enter_context(nc.sbuf_tensor("negstab", [1, 16], F32))
        stabrow = es.enter_context(nc.sbuf_tensor("stabrow", [1, 96], F32))
        rowmaxes = es.enter_context(nc.sbuf_tensor("rowmaxes", [128, 16], F32))
        rmts = es.enter_context(nc.sbuf_tensor("rmts", [128, 32], F32))
        stat = es.enter_context(nc.sbuf_tensor("stat", [128, 20], F32))
        negc = es.enter_context(nc.sbuf_tensor("negc", [128, 12], F32))
        iot = es.enter_context(nc.sbuf_tensor("iot", [128, 128], F32))
        piot = es.enter_context(nc.sbuf_tensor("piot", [128, 1], F32))
        ksumc = es.enter_context(nc.sbuf_tensor("ksumc", [128, 32], BF16))
        scr = es.enter_context(nc.sbuf_tensor("scr", [65, 64], BF16))
        stabhl = es.enter_context(nc.sbuf_tensor("stabhl", [1, 160], BF16))
        stabf = es.enter_context(nc.sbuf_tensor("stabf", [1, 16], F32))
        cpool = es.enter_context(nc.sbuf_tensor("cpool", [128, 4], F32))
        psA = es.enter_context(nc.psum_tensor("ps", [128, 4096], F32))
        SEM_NAMES = ("cT", "cV", "cS", "cG",
                     "sLDX", "sLDPM", "sLDSH", "sWG0", "sWG1",
                     "sWD0", "sWD1", "sWD2", "sWD3",
                     "sOR0", "sOR1", "sOUT", "sSP", "sAGW", "sAGK",
                     "sRK", "sST", "sX1", "sXS2", "sX2P0", "sX2P1",
                     "sTPX0", "sTPX1", "sTPH0", "sTPH1", "sTPL", "sTPQ",
                     "sTPK", "sTPW", "sQK0", "sQK1", "sWV", "cc1", "cc2",
                     "sXS20", "sXS21", "sDBG", "sWDC")
        sems = {n: es.enter_context(nc.semaphore(n)) for n in SEM_NAMES}
        block = es.enter_context(nc.Block())
        ar_t = ARN.ap().tensor
        PSTR = ARN.ap().ap[0][0]

        def b16v(col, width, rows=128):
            assert col + width <= ARENA, (col, width)
            return bass.AP(ar_t, col, [[PSTR, rows], [1, width]])

        def f32v(col, width, rows=128):
            assert col % 2 == 0 and col + 2 * width <= ARENA, (col, width)
            return bass.AP(ar_t, col, [[PSTR, rows], [1, 2 * width]]).bitcast(F32)

        def f32bc(col, rows, B):
            ap = f32v(col, B, rows)
            return bass.AP(ap.tensor, ap.offset,
                           [[ap.ap[0][0], rows], [1, B], [0, 16]])

        def bank(i, cols=512, rows=128, off=0):
            return psA[0:rows, 512 * i + off: 512 * i + off + cols]

        cnt = {k: 0 for k in sems}
        _ENG = {"cT", "cV", "cS", "cG"}

        def inc(inst, ch):
            step = 1 if (ch in _ENG or ch.startswith("cc")) else 16
            inst.then_inc(sems[ch], step)
            cnt[ch] += step
            return cnt[ch]

        def wt(eng, ch, val):
            if val > 0:
                eng.wait_ge(sems[ch], val)

        @block.sync
        def _(s):
            xdst = f32v(X0, NT * 1024).rearrange("p (t d) -> p t d", d=1024)
            inc(s.dma_start(out=xdst, in_=x_d[:, :].rearrange(
                "(t p) d -> p t d", p=128)), "sLDX")
        ld_x = cnt["sLDX"]

        @block.sync
        def _(s):
            inc(s.dma_start(out=f32v(PML, 2048, rows=HD), in_=pmt_d[:, :]),
                "sLDPM")
        ld_pm = cnt["sLDPM"]

        nc.const_aps.register(F32, EPS_LN, cpool[:, 0:1])
        nc.const_aps.register(F32, 0.0, cpool[:, 1:2])
        nc.const_aps.register(F32, 1.0, cpool[:, 2:3])

        @block.vector
        def _(v):
            v.memset(cpool[:, 0:1], EPS_LN)
            v.memset(cpool[:, 1:2], 0.0)
            v.memset(cpool[:, 2:3], 1.0)
            v.memset(cpool[:, 3:4], -0.0)
            v.memset(onesf[:, :], 1.0)
            v.memset(epsrow[:, :], EPS_Z)
            v.memset(nhalf[:, :], 0.0)
            v.memset(pm2[:, :], 0.0)
            v.memset(rowmaxes[:, :], -3.0e38)
            v.drain()
            v.memset(nhalf[0:64, 0:1], -0.5)
            inc(v.memset(nhalf[64:128, 1:2], -0.5), "cV")
        consts_v = cnt["cV"]

        @block.gpsimd
        def _(g):
            g.iota(iot[:, :], [[1, 128]], channel_multiplier=0,
                   allow_small_or_imprecise_dtypes=True)
            inc(g.iota(piot[:, :], [[0, 1]], channel_multiplier=1,
                       allow_small_or_imprecise_dtypes=True), "cG")
        iota_g = cnt["cG"]

        @block.vector
        def _(v):
            wt(v, "cG", iota_g)
            v.tensor_single_scalar(identf[:, :], iot[:, :], piot[:, 0:1],
                                   ALU.is_equal)
            v.drain()
            inc(v.tensor_copy(identb[:, :], identf[:, :]), "cV")
        ident_v = cnt["cV"]

        @block.scalar
        def _(sc):
            wt(sc, "sLDPM", ld_pm)
            wt(sc, "cV", consts_v)
            pml = f32v(PML, 2048, rows=HD)
            for p in range(8):
                sc.mul(pm2[0:HD, p * FF: p * FF + F2],
                       pml[:, (2 * p) * F2:(2 * p) * F2 + F2], 0.125)
                last = sc.mul(pm2[64:64 + HD, p * FF + F2:(p + 1) * FF],
                              pml[:, (2 * p + 1) * F2:(2 * p + 1) * F2 + F2],
                              0.125)
            inc(last, "cS")
        pm2_s = cnt["cS"]

        qdq_prev = {0: 0, 1: 0}

        def emit_qdq(src, dst, W, rows=128, rot=0, waits=()):
            base = QDQ_SC + rot * 4480
            um = f32v(base, W, rows)
            am = f32v(base + 2048, W, rows)
            mcol, icol = base + 4096, base + 4224
            B = W // 16
            mm_ = f32v(mcol, B, rows)
            im_ = f32v(icol, B, rows)
            sm = f32v(QSM + rot * 2048, W, rows)
            src3 = src.rearrange("p (b c) -> p b c", c=16)
            prev_g = qdq_prev[rot]
            I32 = mybir.dt.int32

            @block.vector
            def _(v):
                wt(v, "cG", prev_g)
                for ch, val in waits:
                    wt(v, ch, val)
                v.drain()
                v.tensor_single_scalar(sm.bitcast(I32), src.bitcast(I32),
                                       cpool[0:rows, 3:4].bitcast(I32),
                                       ALU.bitwise_and)
                v.tensor_reduce(mm_, src3, AX.X, ALU.max,
                                apply_absolute_value=True)
                v.drain()
                v.tensor_scalar(mm_, mm_, 1.0 / 12.0, 0.5e-12, ALU.mult,
                                ALU.max)
                v.drain()
                v.reciprocal(im_, mm_)
                v.drain()
                inc(v.tensor_scalar_mul(im_, im_, 2.0), "cV")
            minv_v = cnt["cV"]

            @block.gpsimd
            def _(g):
                wt(g, "cV", minv_v)
                g.tensor_tensor(
                    um.rearrange("p (b c) -> p b c", c=16), src3,
                    f32bc(icol, rows, B), ALU.mult)
                g.drain()
                inc(g.tensor_tensor(um, um, um, ALU.mult), "cG")
            ns_g = cnt["cG"]

            @block.vector
            def _(v):
                wt(v, "cG", ns_g)
                v.drain()
                v._custom_dve(QDQ_A, out=am, in0=um, s0=9.0, s1=25.0,
                              imm2=49.0)
                v.drain()
                with nc.allow_low_precision(reason="T ladder ints exact"):
                    v._custom_dve(QDQ_T, out=dst, in0=um, in1=am, s0=100.0,
                                  s1=196.0, imm2=400.0)
                v.drain()
                inc(v.tensor_tensor(sm.bitcast(I32), sm.bitcast(I32),
                                    f32bc(mcol, rows, B).bitcast(I32),
                                    ALU.bitwise_xor), "cV")
            lad_v = cnt["cV"]

            @block.gpsimd
            def _(g):
                wt(g, "cV", lad_v)
                g.drain()
                inc(g.tensor_tensor(dst, dst, sm, ALU.mult), "cG")
            qdq_prev[rot] = cnt["cG"]
            return cnt["cG"]

        tp_prev = 0

        def prep_gateup(w_hbm, sec_off, tp_prev):
            ld = []

            @block.sync
            def _(s):
                for j, (r0, P) in enumerate(((0, 128), (128, 128), (256, 96))):
                    inc(s.dma_start(
                        out=f32v(WSH + j * 2048, 1024, rows=P),
                        in_=w_hbm[r0:r0 + P, :]), "sLDSH")
            ld_total = cnt["sLDSH"]
            for j, (r0, P) in enumerate(((0, 128), (128, 128), (256, 96))):
                src = f32v(WSH + j * 2048, 1024, rows=P)
                q = emit_qdq(
                    src, bass.AP(ar_t, WSHQ, [[PSTR, P], [1, 1024]]),
                    1024, rows=P, rot=j % 2,
                    waits=(("sLDSH", ld_total), ("sTPW", tp_prev)))

                @block.scalar
                def _(sc):
                    wt(sc, "cG", q)
                    wt(sc, "sAGW", cnt["sAGW"])
                    src_t = bass.AP(ar_t, WSHQ, [[PSTR, P], [128, 8], [1, 128]])
                    dst_t = bass.AP(ar_t, WSHT + j * 128,
                                    [[PSTR, 128], [ISH, 8], [1, P]])
                    inc(sc.dma_start_transpose(dst_t, src_t),
                        "sTPW")
                tp_prev = cnt["sTPW"]

            @block.sync
            def _(s):
                wt(s, "sTPW", tp_prev)
                inc(s.dma_start(out=agw_in.ap()[:, sec_off:sec_off + WPART],
                                in_=b16v(WSHT, WPART)), "sAGW")
            return tp_prev

        tp_prev = prep_gateup(wg_d, 0, tp_prev)
        tp_prev = prep_gateup(wu_d, WPART, tp_prev)

        @block.sync
        def _(s):
            wt(s, "cG", qdq_prev[0])
            wt(s, "cG", qdq_prev[1])
            inc(s.dma_start(out=f32v(WSH, 2816), in_=wd_d[:, :]), "sLDSH")
        ld_wd = cnt["sLDSH"]
        for ci, (cpos, cw) in enumerate(((0, 1024), (1024, 1024),
                                         (2048, 768))):
            q = emit_qdq(f32v(WSH, 2816)[:, cpos:cpos + cw],
                         b16v(WSHQ, 2816)[:, cpos:cpos + cw], cw,
                         rot=ci % 2,
                         waits=(("sLDSH", ld_wd), ("sTPW", tp_prev)))

        @block.scalar
        def _(sc):
            wt(sc, "cG", q)
            wt(sc, "sAGW", cnt["sAGW"])
            src_t = bass.AP(ar_t, WSHQ, [[PSTR, 128], [128, 22], [1, 128]])
            dst_t = bass.AP(ar_t, WSHT, [[PSTR, 128], [128, 22], [1, 128]])
            inc(sc.dma_start_transpose(dst_t, src_t), "sTPW")
        wd_tp = cnt["sTPW"]

        @block.sync
        def _(s):
            wt(s, "sTPW", wd_tp)
            inc(s.dma_start(out=agw_in.ap()[:, 2 * WPART:3 * WPART],
                            in_=b16v(WSHT, WPART)), "sAGW")
        st_d = cnt["sAGW"]

        @block.gpsimd
        def _(g):
            wt(g, "sAGW", st_d)
            inc(g.collective_compute(
                "AllGather", ALU.bypass, replica_groups=[list(range(NCORE))],
                ins=[agw_in.ap().opt()], outs=[agw_out.ap().opt()]), "cc1")
        agw_done = cnt["cc1"]

        @block.sync
        def _(s):
            wt(s, "cc1", agw_done)
            ag_t = agw_out.ap().tensor
            for r in range(8):
                hb2, rr = r // 4, r % 4
                src = bass.AP(ag_t, r * 128 * WRANK + 2 * WPART,
                              [[WRANK, 128], [128, 22], [1, 128]])
                dst = bass.AP(wdc.ap().tensor, hb2 * 22 * 128 * 512 + rr * 128,
                              [[512, 128], [128 * 512, 22], [1, 128]])
                inc(s.dma_start(out=dst, in_=src), "sWDC")
        wdc_done = cnt["sWDC"]

        st8 = {"resid_t": 0, "resid_v": 0}

        def emit_ffn(kind):
            x_dram = (kind == 3)
            wslot_t = {0: 0, 1: 0}
            wdslot_t = {}
            bankev = {0: 0, 1: 0}
            silu_used = {0: 0, 1: 0}
            hused = {}
            lnst_prev = 0
            down_prev_t = 0
            outdma = {0: 0, 1: 0}
            hq_tp = {0: 0, 1: 0, 2: 0, 3: 0}
            def prep_tiles(tiles, xq_tp):
                nonlocal lnst_prev
                for jt, t in enumerate(tiles):
                    rot = jt % 2
                    if x_dram:
                        xt = f32v(X0 + rot * 2048, 1024)
                        xch = "sX2P%d" % rot

                        @block.sync
                        def _(s):
                            wt(s, "cV", lnst_prev)
                            inc(s.dma_start(
                                out=xt,
                                in_=xsp2.ap()[t * 128:(t + 1) * 128, :]), xch)
                        x_wait = (xch, cnt[xch])
                    else:
                        xt = f32v(X0 + t * 2048, 1024)
                        x_wait = ("sLDX", ld_x)
                    lnr = f32v(LNR + rot * 2048, 1024)

                    @block.vector
                    def _(v):
                        wt(v, x_wait[0], x_wait[1])
                        wt(v, "cT", st8["resid_t"])
                        v.drain()
                        v.bn_stats(stat[:, 0:6], xt[:, 0:512])
                        v.bn_stats(stat[:, 6:12], xt[:, 512:1024])
                        v.drain()
                        inc(v.bn_aggr(stat[:, 12:14], stat[:, 0:12]), "cV")
                    lnst_prev = cnt["cV"]

                    @block.scalar
                    def _(sc):
                        wt(sc, "cV", lnst_prev)
                        inc(sc.activation(stat[:, 14:15], stat[:, 13:14],
                                          AF.Sqrt, bias=EPS_LN), "cS")
                    sq_s = cnt["cS"]

                    @block.vector
                    def _(v):
                        wt(v, "cS", sq_s)
                        v.drain()
                        v.reciprocal(stat[:, 15:16], stat[:, 14:15])
                        v.drain()
                        inc(v.tensor_scalar(lnr, xt, stat[:, 12:13],
                                            stat[:, 15:16], ALU.subtract,
                                            ALU.mult), "cV")
                    ln_v = cnt["cV"]

                    xqt = b16v(XQR + rot * 1024, 1024)
                    tpch = "sTPX%d" % rot
                    q = emit_qdq(lnr, xqt, 1024, rot=rot,
                                 waits=(("cV", ln_v), (tpch, xq_tp[rot])))

                    @block.scalar
                    def _(sc):
                        wt(sc, "cG", q)
                        wt(sc, "cT", wslot_t[0])
                        wt(sc, "cT", wslot_t[1])
                        src_t = bass.AP(ar_t, XQR + rot * 1024,
                                        [[PSTR, 128], [128, 8], [1, 128]])
                        dst_t = bass.AP(ar_t, XQT + jt * 128,
                                        [[PSTR, 128], [NTQ * 128, 8], [1, 128]])
                        inc(sc.dma_start_transpose(dst_t, src_t),
                            tpch)
                    xq_tp[rot] = cnt[tpch]

            xq_tp_q = {0: {0: 0, 1: 0}}
            prep_tiles(list(range(NTQ)), xq_tp_q[0])
            for tq in range(NQ):
                tiles = list(range(tq * NTQ, (tq + 1) * NTQ))
                xq_tp = xq_tp_q[tq]

                for r in range(8):
                    sl = r % 2

                    wch = "sWG%d" % sl

                    @block.sync
                    def _(s):
                        wt(s, "cc1", agw_done)
                        wt(s, "cT", wslot_t[sl])
                        inc(s.dma_start(out=b16v(WBUF + sl * 2816, 2816),
                                        in_=agw_out.ap()[r, :, 0:WPART]), wch)
                        inc(s.dma_start(
                            out=b16v(WBUF + (2 + sl) * 2816, 2816),
                            in_=agw_out.ap()[r, :, WPART:2 * WPART]), wch)
                    wld = cnt[wch]
                    for jt, t in enumerate(tiles):
                        bA = (jt % 2) * 2
                        bB = bA + 1
                        hcol = (HR if jt < 2 else HR2) + (jt % 2) * 1408

                        @block.tensor
                        def _(tt):
                            wt(tt, wch, wld)
                            wt(tt, "sTPX0", xq_tp[0])
                            wt(tt, "sTPX1", xq_tp[1])
                            wt(tt, "cV", bankev[jt % 2])
                            for k in range(8):
                                lhs = b16v(XQT + k * NTQ * 128 + jt * 128, 128)
                                tt.ldweights(lhs)
                                tt.matmul(bank(bA, 352), lhs,
                                          b16v(WBUF + sl * 2816 + k * 352, 352),
                                          start=(k == 0), stop=(k == 7))
                                last = tt.matmul(
                                    bank(bB, 352), lhs,
                                    b16v(WBUF + (2 + sl) * 2816 + k * 352, 352),
                                    start=(k == 0), stop=(k == 7))
                            inc(last, "cT")
                        wslot_t[sl] = cnt["cT"]
                        mm_t = cnt["cT"]

                        @block.scalar
                        def _(sc):
                            wt(sc, "cT", mm_t)
                            wt(sc, "cV", silu_used[sl])
                            inc(sc.activation(f32v(SILU + sl * 704, 352),
                                              bank(bA, 352), AF.Silu), "cS")
                        si_s = cnt["cS"]

                        @block.vector
                        def _(v):
                            wt(v, "cS", si_s)
                            wt(v, "cG", hused.get((jt, r % 2), 0))
                            v.drain()
                            hb = f32v(hcol, 704)
                            inc(v.tensor_tensor(
                                hb[:, (r % 2) * 352:(r % 2) * 352 + 352],
                                bank(bB, 352), f32v(SILU + sl * 704, 352),
                                ALU.mult), "cV")
                        bankev[jt % 2] = cnt["cV"]
                        silu_used[sl] = cnt["cV"]
                        hev = cnt["cV"]

                        if r % 2 == 1:
                            hb = f32v(hcol, 704)
                            hqcol = ((HQR if jt < 2 else HQ23)
                                     + (jt % 2) * 2816)
                            hq = b16v(hqcol, 2816)
                            q = emit_qdq(hb, hq[:, (r - 1) * 352:(r + 1) * 352],
                                         704, rot=jt % 2,
                                         waits=(("cV", hev),
                                                ("sTPH%d" % (jt % 2),
                                                 hq_tp[jt])))
                            hused[(jt, 0)] = cnt["cG"]
                            hused[(jt, 1)] = cnt["cG"]
                            if r == 7:
                                @block.scalar
                                def _(sc):
                                    wt(sc, "cG", q)
                                    wt(sc, "cT", down_prev_t)
                                    src_t = bass.AP(
                                        ar_t, hqcol,
                                        [[PSTR, 128], [128, 22], [1, 128]])
                                    dst_t = bass.AP(
                                        ar_t, HQT + jt * 128,
                                        [[PSTR, 128], [NTQ * 128, 22], [1, 128]])
                                    inc(sc.dma_start_transpose(
                                        dst_t, src_t),
                                        "sTPH%d" % (jt % 2))
                                hq_tp[jt] = cnt["sTPH%d" % (jt % 2)]

                if tq + 1 < NQ:
                    xq_tp_q[tq + 1] = {0: xq_tp[0], 1: xq_tp[1]}
                    prep_tiles(list(range((tq + 1) * NTQ, (tq + 2) * NTQ)),
                               xq_tp_q[tq + 1])

                for hb2 in range(2):
                    for icp in range(11):
                        wslot = (icp % 2) * 2
                        wdch = "sWD%d" % wslot

                        @block.sync
                        def _(s):
                            wt(s, "sWDC", wdc_done)
                            wt(s, "cT", wdslot_t.get(wslot, 0))
                            src = bass.AP(
                                wdc.ap().tensor,
                                (hb2 * 22 + 2 * icp) * 128 * 512,
                                [[512, 128], [128 * 512, 2], [1, 512]])
                            inc(s.dma_start(
                                out=b16v(WDB + wslot * 512, 1024).rearrange(
                                    "p (i c) -> p i c", i=2),
                                in_=src), wdch)
                        wdld = cnt[wdch]

                        for ici in range(2):
                            ic = 2 * icp + ici

                            @block.tensor
                            def _(tt):
                                wt(tt, wdch, wdld)
                                wt(tt, "sTPH0", max(hq_tp[0], hq_tp[2]))
                                wt(tt, "sTPH1", max(hq_tp[1], hq_tp[3]))
                                if ic == 0:
                                    wt(tt, "cV", st8["resid_v"])
                                for jt in range(NTQ):
                                    lhs = b16v(HQT + ic * NTQ * 128
                                               + jt * 128, 128)
                                    tt.ldweights(lhs)
                                    last = tt.matmul(
                                        bank(4 + jt), lhs,
                                        b16v(WDB + (wslot + ici) * 512, 512),
                                        start=(ic == 0), stop=(ic == 21))
                                inc(last, "cT")
                            wdslot_t[wslot] = cnt["cT"]
                    dn_t = cnt["cT"]
                    down_prev_t = dn_t

                    for jt, t in enumerate(tiles):
                        if kind == 1:
                            @block.vector
                            def _(v):
                                wt(v, "cT", dn_t)
                                xsl = f32v(X0 + t * 2048 + hb2 * 1024, 512)
                                inc(v.scalar_tensor_tensor(
                                    xsl, bank(4 + jt), 0.5, xsl, ALU.mult,
                                    ALU.add), "cV")
                            st8["resid_v"] = cnt["cV"]
                        else:
                            orch = "sOR%d" % (t % 2)

                            @block.sync
                            def _(s):
                                wt(s, "cV", st8["resid_v"])
                                wt(s, "sOUT", cnt["sOUT"])
                                inc(s.dma_start(
                                    out=f32v(OUTR + (t % 2) * 1024, 512),
                                    in_=xsp2.ap()[t * 128:(t + 1) * 128,
                                                  hb2 * 512:(hb2 + 1) * 512]),
                                    orch)
                            x2l = cnt[orch]

                            @block.vector
                            def _(v):
                                wt(v, "cT", dn_t)
                                wt(v, orch, x2l)
                                orow = f32v(OUTR + (t % 2) * 1024, 512)
                                inc(v.scalar_tensor_tensor(
                                    orow, bank(4 + jt), 0.5, orow, ALU.mult,
                                    ALU.add), "cV")
                            st8["resid_v"] = cnt["cV"]
                            ev_v = cnt["cV"]

                            @block.sync
                            def _(s):
                                wt(s, "cV", ev_v)
                                inc(s.dma_start(
                                    out=out_d[t * 128:(t + 1) * 128,
                                              hb2 * 512:(hb2 + 1) * 512],
                                    in_=f32v(OUTR + (t % 2) * 1024, 512)),
                                    "sOUT")
                            outdma[t % 2] = cnt["sOUT"]
                    if kind == 1:
                        st8["resid_t"] = 0
            return

        emit_ffn(1)
        ffn1_v = cnt["cV"]

        if stop_after == "ffn1":
            _epilogue(block, wt, cnt)
            return nc

        lnst_prev = 0
        for t in range(NT):
            xt = f32v(X0 + t * 2048, 1024)

            @block.sync
            def _(s):
                wt(s, "cV", ffn1_v)
                inc(s.dma_start(out=xsp1.ap()[t * 128:(t + 1) * 128, :],
                                in_=xt), "sSP")

            @block.vector
            def _(v):
                wt(v, "cV", ffn1_v)
                v.drain()
                v.bn_stats(stat[:, 0:6], xt[:, 0:512])
                v.bn_stats(stat[:, 6:12], xt[:, 512:1024])
                v.drain()
                inc(v.bn_aggr(stat[:, 12:14], stat[:, 0:12]), "cV")
            lnst_prev = cnt["cV"]

            @block.scalar
            def _(sc):
                wt(sc, "cV", lnst_prev)
                inc(sc.activation(stat[:, 14:15], stat[:, 13:14], AF.Sqrt,
                                  bias=EPS_LN), "cS")
            sq_s = cnt["cS"]

            @block.vector
            def _(v):
                wt(v, "cS", sq_s)
                v.drain()
                v.reciprocal(stat[:, 15:16], stat[:, 14:15])
                v.drain()
                with nc.allow_low_precision(reason="ln2 bf16 rows (as twin)"):
                    inc(v.tensor_scalar(b16v(A_LN2B + t * 1024, 1024), xt,
                                        stat[:, 12:13], stat[:, 15:16],
                                        ALU.subtract, ALU.mult), "cV")
            ln_v = cnt["cV"]
        ln2b_v = cnt["cV"]

        @block.scalar
        def _(sc):
            wt(sc, "cV", ln2b_v)
            wt(sc, "sSP", cnt["sSP"])
            for t in range(NT):
                src_t = bass.AP(ar_t, A_LN2B + t * 1024,
                                [[PSTR, 128], [128, 8], [1, 128]])
                dst_t = bass.AP(ar_t, A_LN2T + t * 128,
                                [[PSTR, 128], [TOK, 8], [1, 128]])
                inc(sc.dma_start_transpose(dst_t, src_t), "sTPL")
        ln2T_tp = cnt["sTPL"]

        @block.gpsimd
        def _(g):
            wt(g, "cV", ln2b_v)
            inc(g.dma_start(
                out=b16v(A_WV, 8192).rearrange("p (k c) -> p k c", k=8),
                in_=wqkv_d[:, 2048:3072].rearrange("(k p) c -> p k c", p=128)),
                "sWV")
        wv_ld = cnt["sWV"]

        qkev = {0: 0, 1: 0}
        wqk_used = {0: 0, 1: 0}
        for ck in range(16):
            sl = ck % 2

            qch = "sQK%d" % sl

            @block.gpsimd
            def _(g):
                wt(g, "cT", wqk_used[sl])
                inc(g.dma_start(
                    out=b16v(A_WQK + sl * 1024, 1024).rearrange(
                        "p (k c) -> p k c", k=8),
                    in_=wqkv_d[:, ck * 128:(ck + 1) * 128].rearrange(
                        "(k p) c -> p k c", p=128)), qch)
            wqk_ld = cnt[qch]

            @block.tensor
            def _(tt):
                wt(tt, qch, wqk_ld)
                wt(tt, "sTPL", ln2T_tp)
                wt(tt, "cV", qkev[sl])
                grp = sl * 4
                for k in range(8):
                    lhs = b16v(A_WQK + sl * 1024 + k * 128, 128)
                    tt.ldweights(lhs)
                    for tb in range(TB):
                        last = tt.matmul(bank(grp + tb), lhs,
                                         b16v(A_LN2T + k * TOK + tb * 512, 512),
                                         start=(k == 0), stop=(k == 7))
                inc(last, "cT")
            wqk_used[sl] = cnt["cT"]
            qk_t = cnt["cT"]

            @block.vector
            def _(v):
                wt(v, "cT", qk_t)
                grp = sl * 4
                base = (A_Q + ck * TOK) if ck < 8 else (A_K + (ck - 8) * TOK)
                for tb in range(TB):
                    last = v.tensor_copy(b16v(base + tb * 512, 512),
                                         bank(grp + tb))
                inc(last, "cV")
            qkev[sl] = cnt["cV"]
        qk_done_v = cnt["cV"]

        @block.vector
        def _(v):
            wt(v, "cV", ln2b_v)
            v.memset(b16v(A_KVP, NH * 264, rows=65), 0.0)
            v.drain()
            v.memset(b16v(A_V65, NT * 1040), 0.0)
            v.drain()
            ones_ap = bass.AP(ar_t, A_V65 + 1024,
                              [[PSTR, 128], [1040, NT], [65, 16]])
            inc(v.memset(ones_ap, 1.0), "cV")
        v65i = cnt["cV"]

        vev = dict(qkev)
        for ts in range(NT):
            @block.tensor
            def _(tt):
                wt(tt, "sWV", wv_ld)
                wt(tt, "sTPL", ln2T_tp)
                wt(tt, "cV", vev[ts % 2])
                for k in range(8):
                    lhs = b16v(A_LN2T + k * TOK + ts * 128, 128)
                    tt.ldweights(lhs)
                    for vb in range(2):
                        last = tt.matmul(bank((ts % 2) * 2 + vb), lhs,
                                         b16v(A_WV + k * 1024 + vb * 512, 512),
                                         start=(k == 0), stop=(k == 7))
                inc(last, "cT")
            v_t = cnt["cT"]

            @block.vector
            def _(v):
                wt(v, "cT", v_t)
                wt(v, "cV", v65i)
                for vb in range(2):
                    dst = bass.AP(ar_t, A_V65 + ts * 1040 + vb * 8 * 65,
                                  [[PSTR, 128], [65, 8], [1, 64]])
                    last = v.tensor_copy(dst, bank((ts % 2) * 2 + vb).rearrange(
                        "p (h d) -> p h d", d=64))
                inc(last, "cV")
            vev[ts % 2] = cnt["cV"]
        v_done_v = cnt["cV"]

        upar = {6: v_done_v, 7: v_done_v}
        for p in range(8):
            for ts in range(NT):
                ub = 6 + (ts % 2)

                @block.tensor
                def _(tt):
                    wt(tt, "cS", pm2_s)
                    wt(tt, "cV", qk_done_v)
                    wt(tt, "cV", upar[ub])
                    lhs = b16v(A_K + p * TOK + ts * 128, 128)
                    tt.ldweights(lhs)
                    inc(tt.matmul(bank(ub, 256), lhs,
                                  pm2[:, p * FF:(p + 1) * FF],
                                  start=True, stop=True), "cT")
                u_t = cnt["cT"]

                @block.vector
                def _(v):
                    wt(v, "cT", u_t)
                    v.drain()
                    inc(v.tensor_reduce(
                        rmts[:, 2 * ts:2 * ts + 2], bank(ub, 256).rearrange(
                            "p (h f) -> p h f", f=128), AX.X, ALU.max), "cV")
                upar[ub] = cnt["cV"]

            @block.vector
            def _(v):
                v.drain()
                src = bass.AP(rmts.ap().tensor, 0, [[32, 128], [1, 2], [2, NT]])
                inc(v.tensor_reduce(rowmaxes[:, 2 * p:2 * p + 2], src,
                                    AX.X, ALU.max), "cV")
        rmax_v = cnt["cV"]

        @block.tensor
        def _(tt):
            wt(tt, "cV", rmax_v)
            wt(tt, "cV", ident_v)
            inc(tt.transpose(psA[0:16, 3072:3200], rowmaxes[:, :],
                             identf[:, :]), "cT")
        rtp_t = cnt["cT"]

        @block.vector
        def _(v):
            wt(v, "cT", rtp_t)
            v.tensor_reduce(stat[0:16, 16:17], psA[0:16, 3072:3200], AX.X,
                            ALU.max)
            v.drain()
            inc(v.tensor_scalar_mul(stat[0:16, 17:18], stat[0:16, 16:17],
                                    -1.0), "cV")
        nstc_v = cnt["cV"]

        @block.tensor
        def _(tt):
            wt(tt, "cV", nstc_v)
            inc(tt.transpose(psA[0:1, 3072:3088], stat[0:16, 17:18],
                             identf[0:16, 0:16]), "cT")
        nst_t = cnt["cT"]

        @block.vector
        def _(v):
            wt(v, "cT", nst_t)
            v.tensor_copy(negstab[0:1, 0:16], psA[0:1, 3072:3088])
            v.drain()
            with nc.allow_low_precision(reason="hi/lo bf16 pair is exact"):
                v.tensor_copy(stabhl[0:1, 0:16], negstab[0:1, 0:16])
            v.drain()
            v.tensor_copy(stabf[0:1, :], stabhl[0:1, 0:16])
            v.drain()
            with nc.allow_low_precision(reason="hi/lo bf16 pair is exact"):
                inc(v.tensor_tensor(stabhl[0:1, 16:32], negstab[0:1, 0:16],
                                    stabf[0:1, :], ALU.subtract), "cV")
        negstab_v = cnt["cV"]

        KF4 = A_LN2T
        AGB = 24576
        args_done = {0: 0, 1: 0, 2: 0, 3: 0}
        argb_used = {6: 0, 7: 0}
        kf_used = {0: 0, 1: 0, 2: 0, 3: 0}
        kv_read_v = 0
        q2_used = 0
        for p in range(8):
            @block.vector
            def _(v):
                wt(v, "cV", v_done_v)
                wt(v, "cT", q2_used)
                v.drain()
                inc(v.tensor_tensor(b16v(A_Q2, TOK), b16v(A_K + p * TOK, TOK),
                                    b16v(A_K + p * TOK, TOK), ALU.mult), "cV")
            k2_s = cnt["cV"]

            st_c2 = {}

            def emit_u2(ts, p=p, k2_s=k2_s):
                ub = ts % 4
                bb = 6 + (ts % 2)

                @block.tensor
                def _(tt):
                    wt(tt, "cV", k2_s)
                    wt(tt, "cV", args_done[ub])
                    wt(tt, "cV", negstab_v)
                    lhs = b16v(A_K + p * TOK + ts * 128, 128)
                    tt.ldweights(lhs)
                    tt.matmul(bank(ub, 256), lhs, pm2[:, p * FF:(p + 1) * FF],
                              start=True, stop=False)
                    lhs2 = b16v(A_Q2 + ts * 128, 128)
                    tt.ldweights(lhs2)
                    tt.matmul(bank(ub, 2, off=256), lhs2, nhalf[:, :],
                              start=False, stop=False)
                    inc(tt.matmul(bank(ub, 2, off=256), onesf[0:1, 0:128],
                                  negstab[0:1, 2 * p:2 * p + 2],
                                  start=False, stop=True), "cT")
                st_c2[("u2", ts)] = cnt["cT"]

                agb = AGB + (bb - 6) * 1024

                @block.vector
                def _(v):
                    wt(v, "cT", st_c2[("u2", ts)])
                    wt(v, "cS", argb_used[bb])
                    v.drain()
                    for hh in range(2):
                        nd = bank(ub, 1, off=256 + hh)
                        v.tensor_scalar(f32v(agb + hh * 512, 128),
                                        bank(ub, 128, off=hh * 128), nd,
                                        None, ALU.add)
                        v.tensor_scalar(f32v(agb + hh * 512 + 256, 128),
                                        bank(ub, 128, off=hh * 128), nd,
                                        -1.0, ALU.subtract, ALU.mult)
                    inc(v.drain(), "cV")
                args_done[ub] = cnt["cV"]
                st_c2[("args", ts)] = cnt["cV"]

                @block.scalar
                def _(sc):
                    wt(sc, "cV", st_c2[("args", ts)])
                    wt(sc, "cT", kf_used[ub])
                    inc(sc.activation(b16v(KF4 + ub * 512, 512),
                                      f32v(agb, 512), AF.Exp), "cS")
                argb_used[bb] = cnt["cS"]
                st_c2[("kf", ts)] = cnt["cS"]

            def emit_kv(ts, p=p):
                ub = ts % 4

                @block.tensor
                def _(tt):
                    wt(tt, "cS", st_c2[("kf", ts)])
                    if ts == 0:
                        wt(tt, "cV", kv_read_v)
                    for hh in range(2):
                        lhs = bass.AP(ar_t,
                                      A_V65 + ts * 1040 + (2 * p + hh) * 65,
                                      [[PSTR, 128], [1, 65]])
                        tt.ldweights(lhs)
                        last = tt.matmul(bank(4 + hh, 256, rows=65), lhs,
                                         b16v(KF4 + ub * 512 + hh * 256, 256),
                                         start=(ts == 0), stop=(ts == NT - 1))
                    inc(last, "cT")
                kf_used[ub] = cnt["cT"]

            for ts in range(NT):
                emit_u2(ts)
                if ts >= 3:
                    emit_kv(ts - 3)
            for ts in range(NT - 3, NT):
                emit_kv(ts)
            q2_used = cnt["cT"]
            kv_t = cnt["cT"]

            @block.vector
            def _(v):
                wt(v, "cT", kv_t)
                for hh in range(2):
                    dst = bass.AP(ar_t, A_KVP + (2 * p + hh) * 264,
                                  [[PSTR, 65], [1, 256]])
                    last = v.tensor_copy(dst, bank(4 + hh, 256, rows=65))
                inc(last, "cV")
            kv_read_v = cnt["cV"]
        kvp_v = cnt["cV"]

        @block.sync
        def _(s):
            wt(s, "cV", kvp_v)
            inc(s.dma_start(out=agk_in.ap()[0:1, 0:KVE].rearrange(
                "a (p w) -> (a p) w", p=65),
                in_=b16v(A_KVP, NH * 264, rows=65)), "sAGK")
            inc(s.dma_start(
                out=agk_in.ap()[0:1, KVE:KVE + 32],
                in_=stabhl[0:1, 0:32]), "sAGK")
        agks = cnt["sAGK"]

        @block.gpsimd
        def _(g):
            wt(g, "sAGK", agks)
            inc(g.collective_compute(
                "AllGather", ALU.bypass,
                replica_groups=[[0, 1, 2, 3], [4, 5, 6, 7]],
                ins=[agk_in.ap().opt()], outs=[agk_out.ap().opt()]), "cc2")
        agk_done = cnt["cc2"]

        @block.sync
        def _(s):
            wt(s, "cc2", agk_done)
            src = bass.AP(agk_out.ap().tensor, agk_out.ap().offset + KVE,
                          [[KVE + 32, NGRP], [1, 32]])
            inc(s.dma_start(out=stabhl[0:1, 32:160], in_=src), "sST")
        stld = cnt["sST"]

        @block.vector
        def _(v):
            wt(v, "sST", stld)
            v.drain()
            hi = bass.AP(stabhl.ap().tensor, 32, [[160, 1], [32, 4], [1, 16]])
            lo = bass.AP(stabhl.ap().tensor, 48, [[160, 1], [32, 4], [1, 16]])
            v.tensor_tensor(stabrow[0:1, 0:64].rearrange(
                "a (r h) -> a r h", h=16), hi, lo, ALU.add)
            v.drain()
            vsrc = bass.AP(stabrow.ap().tensor, 0, [[96, 1], [1, 16], [16, 4]])
            v.tensor_reduce(stabrow[0:1, 64:80], vsrc, AX.X, ALU.min)
            v.drain()
            gb = bass.AP(stabrow.ap().tensor, 64, [[96, 1], [0, 4], [1, 16]])
            dv = stabrow[0:1, 0:64]
            inc(v.tensor_tensor(dv, gb, dv, ALU.subtract), "cV")
        stsc_v = cnt["cV"]

        @block.scalar
        def _(sc):
            wt(sc, "cV", stsc_v)
            inc(sc.activation(stabrow[0:1, 0:64], stabrow[0:1, 0:64], AF.Exp),
                "cS")
        scexp_s = cnt["cS"]

        @block.tensor
        def _(tt):
            wt(tt, "cS", scexp_s)
            inc(tt.matmul(psA[0:65, 3072:3136], onesf[0:1, 0:65],
                          stabrow[0:1, 0:64], start=True, stop=True), "cT")
        scr_t = cnt["cT"]

        @block.vector
        def _(v):
            wt(v, "cT", scr_t)
            inc(v.tensor_copy(scr[:, :], psA[0:65, 3072:3136]), "cV")
        scr_v = cnt["cV"]

        cmb_v = 0
        for r in range(NGRP):
            @block.sync
            def _(s):
                wt(s, "cc2", agk_done)
                wt(s, "cV", cmb_v)
                inc(s.dma_start(
                    out=b16v(A_RANK, NH * 264, rows=65),
                    in_=agk_out.ap()[r, 0:KVE].rearrange("(p w) -> p w", p=65)),
                    "sRK")
            rkld = cnt["sRK"]

            @block.vector
            def _(v):
                wt(v, "sRK", rkld)
                wt(v, "cV", scr_v)
                v.drain()
                scb = bass.AP(scr.ap().tensor, r * 16,
                              [[64, 65], [1, 16], [0, 264]])
                rk3 = bass.AP(ar_t, A_RANK, [[PSTR, 65], [264, 16], [1, 264]])
                v.tensor_tensor(rk3, rk3, scb, ALU.mult)
                v.drain()
                if r == 0:
                    last = v.tensor_copy(b16v(A_KVACC, NH * 264, rows=65),
                                         b16v(A_RANK, NH * 264, rows=65))
                else:
                    last = v.tensor_tensor(b16v(A_KVACC, NH * 264, rows=65),
                                           b16v(A_KVACC, NH * 264, rows=65),
                                           b16v(A_RANK, NH * 264, rows=65),
                                           ALU.add)
                inc(last, "cV")
            cmb_v = cnt["cV"]

        @block.vector
        def _(v):
            wt(v, "cV", cmb_v)
            wt(v, "cV", v_done_v)
            v.drain()
            v.memset(bass.AP(ar_t, A_KVGB, [[PSTR, 80], [1, 4096]]), 0.0)
            v.drain()
            dst = bass.AP(ar_t, A_KVGB, [[PSTR, 65], [256, 16], [1, 256]])
            src = bass.AP(ar_t, A_KVACC, [[PSTR, 65], [264, 16], [1, 256]])
            inc(v.tensor_copy(dst, src), "cV")
        kvgb_v = cnt["cV"]

        @block.vector
        def _(v):
            v.memset(b16v(A_KVT2, 4096), 0.0)
            inc(v.drain(), "cV")
        kvt2_z = cnt["cV"]

        KSTG = A_QF
        @block.scalar
        def _(sc):
            wt(sc, "cV", kvgb_v)
            wt(sc, "cT", kf_used[0])
            wt(sc, "cT", kf_used[1])
            src_t = bass.AP(ar_t, A_KVGB,
                            [[PSTR, 80], [128, 32], [1, 128]])
            dst_t = bass.AP(ar_t, KSTG,
                            [[PSTR, 128], [80, 32], [1, 80]])
            inc(sc.dma_start_transpose(dst_t, src_t), "sTPK")
        kstg_tp = cnt["sTPK"]

        @block.vector
        def _(v):
            wt(v, "sTPK", kstg_tp)
            wt(v, "cV", kvt2_z)
            v.drain()
            for h in range(16):
                pair, hh = h // 2, h % 2
                for fc in range(2):
                    sblk = KSTG + (h * 2 + fc) * 80
                    dcol = A_KVT2 + pair * 512 + (hh * 2 + fc) * 128 + hh * 64
                    last = v.tensor_copy(
                        bass.AP(ar_t, dcol, [[PSTR, 128], [1, 64]]),
                        bass.AP(ar_t, sblk, [[PSTR, 128], [1, 64]]))
            inc(last, "cV")
        kvt2_v = cnt["cV"]
        kvt2_tp = kstg_tp

        @block.vector
        def _(v):
            wt(v, "sTPK", kstg_tp)
            v.drain()
            src = bass.AP(ar_t, KSTG + 64, [[PSTR, 128], [80, 32]])
            inc(v.tensor_copy(ksumc[:, :], src), "cV")
        ksumc_v = cnt["cV"]

        qf_used = {0: 0, 1: 0}
        attn_ev = 0
        expq_s = {6: 0, 7: 0}
        eq_nc_used = {6: 0, 7: 0}
        uargs_done = {6: 0, 7: 0}
        q2q_used = 0
        tp_last_qf = [0]
        for p in range(8):
            @block.vector
            def _(v):
                wt(v, "cV", qk_done_v)
                wt(v, "cT", q2q_used)
                wt(v, "cT", kf_used[0])
                wt(v, "cT", kf_used[1])
                v.drain()
                inc(v.tensor_tensor(b16v(A_Q2, TOK), b16v(A_Q + p * TOK, TOK),
                                    b16v(A_Q + p * TOK, TOK), ALU.mult), "cV")
            q2_s = cnt["cV"]

            qfe_s = 0
            qf_tp_prev = tp_last_qf[0]
            for ts in range(NT):
                ub = 6 + (ts % 2)
                ncol = 8 + 2 * (ub - 6)
                rcol = 2 * (ub - 6)

                @block.tensor
                def _(tt):
                    wt(tt, "cV", q2_s)
                    wt(tt, "cV", uargs_done[ub])
                    lhs = b16v(A_Q + p * TOK + ts * 128, 128)
                    tt.ldweights(lhs)
                    tt.matmul(bank(ub, 256), lhs, pm2[:, p * FF:(p + 1) * FF],
                              start=True, stop=False)
                    lhs2 = b16v(A_Q2 + ts * 128, 128)
                    tt.ldweights(lhs2)
                    inc(tt.matmul(bank(ub, 2, off=256), lhs2, nhalf[:, :],
                                  start=False, stop=True), "cT")
                q2q_used = cnt["cT"]
                uq_t = cnt["cT"]

                earg = EARG + (ub - 6) * 1024

                @block.vector
                def _(v):
                    wt(v, "cT", uq_t)
                    wt(v, "cS", eq_nc_used[ub])
                    v.drain()
                    v.tensor_reduce(rmts[:, rcol:rcol + 2],
                                    bank(ub, 256).rearrange(
                                        "p (h f) -> p h f", f=128),
                                    AX.X, ALU.max)
                    v.drain()
                    v.tensor_tensor(negc[:, ncol:ncol + 2],
                                    bank(ub, 2, off=256),
                                    rmts[:, rcol:rcol + 2],
                                    ALU.subtract)
                    v.drain()
                    for hh in range(2):
                        ncc = negc[:, ncol + hh:ncol + hh + 1]
                        v.tensor_scalar(f32v(earg + hh * 512, 128),
                                        bank(ub, 128, off=hh * 128), ncc,
                                        None, ALU.add)
                        v.tensor_scalar(f32v(earg + hh * 512 + 256, 128),
                                        bank(ub, 128, off=hh * 128), ncc,
                                        -1.0, ALU.subtract, ALU.mult)
                    inc(v.drain(), "cV")
                ncq_v = cnt["cV"]

                @block.scalar
                def _(sc):
                    wt(sc, "cV", ncq_v)
                    wt(sc, "sTPQ", qf_tp_prev)
                    dst3 = bass.AP(ar_t, A_QF + ts * 128,
                                   [[PSTR, 128], [TOK, 4], [1, 128]])
                    inc(sc.activation(dst3, f32v(earg, 512), AF.Exp), "cS")
                expq_s[ub] = cnt["cS"]
                eq_nc_used[ub] = cnt["cS"]
                uargs_done[ub] = ncq_v
            qfe_s = cnt["cS"]

            tp = {}
            for hh in range(2):
                @block.scalar
                def _(sc):
                    wt(sc, "cS", qfe_s)
                    wt(sc, "cT", qf_used[hh])
                    for fc in range(2):
                        src_t = bass.AP(ar_t, A_QF + hh * (2 * TOK) + fc * TOK,
                                        [[PSTR, 128], [128, NT], [1, 128]])
                        dst_t = bass.AP(ar_t,
                                        A_QFT + hh * (2 * TOK) + fc * TOK,
                                        [[PSTR, 128], [128, NT], [1, 128]])
                        inc(sc.dma_start_transpose(dst_t, src_t), "sTPQ")
                tp[hh] = cnt["sTPQ"]
            tp_last_qf[0] = cnt["sTPQ"]

            for tb in range(TB):
                @block.tensor
                def _(tt):
                    wt(tt, "sTPQ", tp[1])
                    wt(tt, "sTPK", kvt2_tp)
                    wt(tt, "cV", kvt2_v)
                    wt(tt, "cV", ksumc_v)
                    wt(tt, "cV", attn_ev)
                    for mt in range(4):
                        hh, fc = mt // 2, mt % 2
                        lhs = b16v(A_KVT2 + p * 512 + mt * 128, 128)
                        tt.ldweights(lhs)
                        tt.matmul(bank(tb), lhs,
                                  b16v(A_QFT + hh * (2 * TOK) + fc * TOK
                                       + tb * 512, 512),
                                  start=(mt == 0), stop=(mt == 3))
                    for hh in range(2):
                        zrows = psA[hh * 64:(hh + 1) * 64,
                                    4 * 512:5 * 512]
                        for fc in range(2):
                            kc = bass.AP(ksumc.ap().tensor,
                                         (2 * p + hh) * 2 + fc,
                                         [[32, 128], [0, 64]])
                            tt.matmul(zrows, kc,
                                      b16v(A_QFT + hh * (2 * TOK) + fc * TOK
                                           + tb * 512, 512),
                                      start=(fc == 0), stop=False)
                        inc(tt.matmul(zrows, onesf[0:1, 0:64],
                                      epsrow[0:1, :], start=False, stop=True),
                            "cT")
                nz_t = cnt["cT"]

                @block.vector
                def _(v):
                    wt(v, "cT", nz_t)
                    v.drain()
                    v.reciprocal(f32v(RZF, 512), bank(4, 512))
                    v.drain()
                    inc(v.tensor_tensor(
                        b16v(A_ATT + p * TOK + tb * 512, 512),
                        bank(tb), f32v(RZF, 512), ALU.mult), "cV")
                attn_ev = cnt["cV"]
            qf_used[0] = cnt["cT"]
            qf_used[1] = cnt["cT"]
        attn_done_v = cnt["cV"]

        if dbg_attn:
            @block.vector
            def _(v):
                wt(v, "cV", attn_done_v)
                v.drain()
                inc(v.tensor_copy(f32v(A_LN2R, 1024),
                                  psA[:, 4 * 512:6 * 512]), "cV")
            dbgz_v = cnt["cV"]

            @block.sync
            def _(s):
                wt(s, "cV", dbgz_v)
                inc(s.dma_start(out=dbga, in_=b16v(A_ATT, 8 * TOK)), "sDBG")
                inc(s.dma_start(out=dbgr, in_=b16v(A_RZT, 512)), "sDBG")
                inc(s.dma_start(out=dbgq, in_=b16v(A_QFT, 4 * TOK)), "sDBG")
                inc(s.dma_start(out=dbgk, in_=b16v(A_KVT2, 4096)), "sDBG")
                inc(s.dma_start(out=dbgc, in_=negc[:, 0:4]), "sDBG")
                inc(s.dma_start(out=dbgs, in_=ksumc[:, :]), "sDBG")
                inc(s.dma_start(out=dbgz, in_=f32v(A_LN2R, 1024)), "sDBG")
            ctx["dbg_done"] = cnt["sDBG"]

        @block.gpsimd
        def _(g):
            wt(g, "cT", cnt["cT"])
            inc(g.dma_start(
                out=b16v(A_WV, 8192).rearrange("p (k c) -> p k c", k=8),
                in_=wout_d.rearrange("(k p) c -> p k c", p=128)), "sWV")
        wout_ld = cnt["sWV"]

        woutev = {0: 0, 1: 0}
        xs2p = {0: 0, 1: 0}
        for ts in range(NT):
            @block.sync
            def _(s):
                wt(s, "cV", attn_done_v)
                wt(s, "cV", woutev[ts % 2])
                wt(s, "sXS2%d" % (ts % 2), xs2p[ts % 2])
                inc(s.dma_start(out=f32v(A_XT + (ts % 2) * 2048, 1024),
                                in_=xsp1.ap()[ts * 128:(ts + 1) * 128, :]),
                    "sX1")
            x1ld = cnt["sX1"]

            @block.tensor
            def _(tt):
                wt(tt, "sWV", wout_ld)
                wt(tt, "cV", attn_done_v)
                wt(tt, "cV", woutev[ts % 2])
                for k in range(8):
                    lhs = b16v(A_ATT + k * TOK + ts * 128, 128)
                    tt.ldweights(lhs)
                    for vb in range(2):
                        last = tt.matmul(bank((ts % 2) * 2 + vb), lhs,
                                         b16v(A_WV + k * 1024 + vb * 512, 512),
                                         start=(k == 0), stop=(k == 7))
                inc(last, "cT")
            wo_t = cnt["cT"]

            @block.vector
            def _(v):
                wt(v, "cT", wo_t)
                wt(v, "sX1", x1ld)
                xb = f32v(A_XT + (ts % 2) * 2048, 1024)
                for vb in range(2):
                    last = v.tensor_tensor(xb[:, vb * 512:(vb + 1) * 512],
                                           xb[:, vb * 512:(vb + 1) * 512],
                                           bank((ts % 2) * 2 + vb), ALU.add)
                inc(last, "cV")
            x2_v = cnt["cV"]

            @block.sync
            def _(s):
                wt(s, "cV", x2_v)
                inc(s.dma_start(out=xsp2.ap()[ts * 128:(ts + 1) * 128, :],
                                in_=f32v(A_XT + (ts % 2) * 2048, 1024)),
                    "sXS2%d" % (ts % 2))
            woutev[ts % 2] = x2_v
            xs2p[ts % 2] = cnt["sXS2%d" % (ts % 2)]
        x2_done = (cnt["sXS20"], cnt["sXS21"])

        @block.vector
        def _(v):
            wt(v, "sXS20", x2_done[0])
            wt(v, "sXS21", x2_done[1])
            if dbg_attn:
                wt(v, "sDBG", ctx["dbg_done"])
            inc(v.drain(), "cV")
        st8["resid_v"] = cnt["cV"]
        st8["resid_t"] = 0

        emit_ffn(3)

        _epilogue(block, wt, cnt)
    mybir.codegen_inst_isa_subclasses(nc)
    return nc


_NC_CACHE = {}


def kernel(L0_attn_x, L0_attn_W_qkv, L0_attn_W_out, L0_attn_proj,
           L0_ffn_w_gate, L0_ffn_w_up, L0_ffn_w_down):
    x = np.ascontiguousarray(np.asarray(L0_attn_x, dtype=np.float32))
    B, N, Dm = x.shape
    TOK = B * N // NCORE
    NT = TOK // 128
    if "nc" not in _NC_CACHE or _NC_CACHE.get("NT") != NT:
        _NC_CACHE["nc"] = build_nc(NT)
        _NC_CACHE["NT"] = NT
    nc = _NC_CACHE["nc"]

    wqkv = np.ascontiguousarray(np.asarray(L0_attn_W_qkv, np.float32))
    wout = np.ascontiguousarray(np.asarray(L0_attn_W_out, np.float32))
    proj = np.asarray(L0_attn_proj, np.float32)
    pmt = np.ascontiguousarray(proj.transpose(2, 0, 1).reshape(HD, NH * F2))
    wg = np.ascontiguousarray(np.asarray(L0_ffn_w_gate, np.float32))
    wu = np.ascontiguousarray(np.asarray(L0_ffn_w_up, np.float32))
    wdn = np.ascontiguousarray(np.asarray(L0_ffn_w_down, np.float32))

    xs = x.reshape(B * N, Dm)
    in_maps = []
    for c in range(NCORE):
        in_maps.append({
            "x": np.ascontiguousarray(xs[c * TOK:(c + 1) * TOK]),
            "wqkv": wqkv,
            "wout": wout,
            "pmt": pmt,
            "wg": np.ascontiguousarray(wg[c * ISH:(c + 1) * ISH]),
            "wu": np.ascontiguousarray(wu[c * ISH:(c + 1) * ISH]),
            "wd": np.ascontiguousarray(wdn[c * 128:(c + 1) * 128]),
        })
    res = run_bass_kernel_spmd(nc, in_maps, core_ids=list(range(NCORE)))
    out = np.concatenate([res.results[c]["out"] for c in range(NCORE)], axis=0)
    return out.reshape(B, N, Dm)


# version 15
# speedup vs baseline: 1.0125x; 1.0125x over previous
import numpy as np
import ml_dtypes

import concourse.bass as bass
import concourse.mybir as mybir
from concourse import dve_ops
from concourse.dve_spec import Spec, Src0, Src1, One, C2, lower
from concourse.dve_table_gen import dve_ver_for
from concourse.bass_utils import run_bass_kernel_spmd

F32 = mybir.dt.float32
BF16 = mybir.dt.bfloat16
ALU = mybir.AluOpType
AX = mybir.AxisListType
AF = mybir.ActivationFunctionType


def _register_qdq_ops():
    have = {o.name: o for o in dve_ops.OPS}
    if "QDQ_A" in have:
        return have["QDQ_A"], have["QDQ_T"]
    from concourse.dve_spec import C0, C1
    u = Src0
    a_body = ((u > One) + (u > C0)) + ((u > C1) + (u > C2))
    c5, c6, c7 = u > C0, u > C1, u > C2
    i1 = c5 + c6
    i2 = c7 + c7
    j = i1 + i2
    t_body = Src1 + (j + j)

    def a_ref(in0, in1, s0, s1, imm2):
        x = in0
        one = np.asarray(1.0, x.dtype)
        zero = np.asarray(0.0, x.dtype)
        r = np.where(x > np.asarray(1.0, x.dtype), one, zero)
        for K in (s0, s1, imm2):
            r = r + np.where(x > np.asarray(K, x.dtype), one, zero)
        return r.astype(x.dtype)

    def t_ref(in0, in1, s0, s1, imm2):
        uu, A = in0, in1
        one = np.asarray(1.0, uu.dtype)
        zero = np.asarray(0.0, uu.dtype)
        j = np.where(uu > np.asarray(s0, uu.dtype), one, zero)
        j = j + np.where(uu > np.asarray(s1, uu.dtype), one, zero)
        j = j + np.asarray(2.0, uu.dtype) * np.where(
            uu > np.asarray(imm2, uu.dtype), one, zero)
        return (A + j + j).astype(uu.dtype)

    ver = dve_ver_for("TRN2")
    spec_a, spec_t = Spec(body=a_body), Spec(body=t_body)
    op_a = dve_ops.DveOp("QDQ_A", spec_a, subdim=False,
                         uops_sha={ver: lower(spec_a, ver=ver, name="QDQ_A").sha(ver)},
                         reference=a_ref)
    op_t = dve_ops.DveOp("QDQ_T", spec_t, subdim=False,
                         uops_sha={ver: lower(spec_t, ver=ver, name="QDQ_T").sha(ver)},
                         reference=t_ref)
    names = [o.name for o in dve_ops.OPS]
    for op, old in ((op_a, "DVEHASH32L_SEED"), (op_t, "DVEHASH32L_BODY")):
        idx = names.index(old)
        dve_ops.OPS[idx] = op
        row = dve_ops._SUB_OPCODE_FOR_NAME.pop(old)
        dve_ops._SUB_OPCODE_FOR_NAME[op.name] = row
        dve_ops.CUSTOM_DVE_SPECS.pop(old, None)
        dve_ops.CUSTOM_DVE_SPECS[op.name] = op.spec
        dve_ops.CUSTOM_DVE_REFS.pop(old, None)
        dve_ops.CUSTOM_DVE_REFS[op.name] = op.ref
    return op_a, op_t


QDQ_A, QDQ_T = _register_qdq_ops()

D = 1024
I = 2816
NH = 16
HD = 64
F2 = 128
FF = 256
NCORE = 8
NGRP = 4
ISH = I // 8
WPART = 8 * ISH
WRANK = 3 * WPART
EPS_LN = 1e-5
EPS_Z = 256e-6
ARENA = 101120

X0 = 0
LNR = 32768
XQR = 36864
QDQ_SC = 38912
QSM = 47872
SILU = 52096
HR = 53504
HQR = 56320
XQT = 61952
HQT = 66048
WBUF = 77312
WDB = 88576
OUTR = 90624
WSH = 92672
HR2 = 92672
HQ23 = 95488
WSHQ = 66048
WSHT = 77312
PML = 88576
A_LN2B = 32768
A_K = 0
A_RANK = 0
A_KVACC = 4224
A_ATT = 0
A_LN2T = 16384
A_QF = 16384
A_QFT = 24576
A_Q = 32768
A_RZT = 65792
RZF = 65792
EARG = 66816
A_XT = 37888
A_V65 = 49152
A_KVGB = 49152
A_KVT2 = 53248
A_Q2 = 57344
A_KVP = 65792
A_WV = 74240
A_LN2R = 82432
A_KF = 86528
A_WQK = 87040


def _epilogue(block, wt, cnt):
    @block.sync
    def _(s):
        for ch in ("sLDX", "sLDPM", "sLDSH", "sSP", "sAGW", "sAGK",
                   "sRK", "sST", "sX1", "sXS20", "sXS21", "sOUT",
                   "sWG0", "sWG1", "sWD0", "sWD1", "sWD2", "sWD3",
                   "sOR0", "sOR1", "sX2P0", "sX2P1"):
            wt(s, ch, cnt[ch])

    @block.scalar
    def _(sc):
        for ch in ("sTPX0", "sTPX1", "sTPH0", "sTPH1", "sTPL", "sTPQ",
                   "sTPK", "sTPW"):
            wt(sc, ch, cnt[ch])
        wt(sc, "cS", cnt["cS"])

    @block.gpsimd
    def _(g):
        for ch in ("sQK0", "sQK1", "sWV", "cc1", "cc2"):
            wt(g, ch, cnt[ch])
        wt(g, "cG", cnt["cG"])

    @block.tensor
    def _(tt):
        wt(tt, "cT", cnt["cT"])

    @block.vector
    def _(v):
        wt(v, "cV", cnt["cV"])


def build_nc(NT=16, stop_after=None, dbg_spill=False, dbg_attn=False):
    TOK = NT * 128
    NTQ = max(NT // 4, 1)
    NQ = NT // NTQ
    TB = TOK // 512
    nc = bass.Bass("TRN2", target_bir_lowering=False)

    x_d = nc.dram_tensor("x", [TOK, D], F32, kind="ExternalInput").ap()
    wqkv_d = nc.dram_tensor("wqkv", [D, 3 * D], F32, kind="ExternalInput").ap()
    wout_d = nc.dram_tensor("wout", [D, D], F32, kind="ExternalInput").ap()
    pmt_d = nc.dram_tensor("pmt", [HD, NH * F2], F32, kind="ExternalInput").ap()
    wg_d = nc.dram_tensor("wg", [ISH, D], F32, kind="ExternalInput").ap()
    wu_d = nc.dram_tensor("wu", [ISH, D], F32, kind="ExternalInput").ap()
    wd_d = nc.dram_tensor("wd", [128, I], F32, kind="ExternalInput").ap()
    out_d = nc.dram_tensor("out", [TOK, D], F32, kind="ExternalOutput").ap()

    agw_in = nc.dram_tensor("agw_in", [128, WRANK], BF16)
    agw_out = nc.dram_tensor("agw_out", [NCORE, 128, WRANK], BF16)
    wdc = nc.dram_tensor("wdc", [2 * 22 * 128 * 512], BF16)
    KVE = 65 * NH * 264
    agk_in = nc.dram_tensor("agk_in", [1, KVE + 32], BF16)
    agk_out = nc.dram_tensor("agk_out", [NGRP, KVE + 32], BF16)
    spk = dict(kind="ExternalOutput") if dbg_spill else {}
    xsp1 = nc.dram_tensor("xsp1", [TOK, D], F32, **spk)
    xsp2 = nc.dram_tensor("xsp2", [TOK, D], F32, **spk)
    if dbg_attn:
        dbga = nc.dram_tensor("dbga", [128, 8 * TOK], BF16,
                              kind="ExternalOutput").ap()
        dbgr = nc.dram_tensor("dbgr", [128, 512], BF16,
                              kind="ExternalOutput").ap()
        dbgq = nc.dram_tensor("dbgq", [128, 4 * TOK], BF16,
                              kind="ExternalOutput").ap()
        dbgk = nc.dram_tensor("dbgk", [128, 4096], BF16,
                              kind="ExternalOutput").ap()
        dbgc = nc.dram_tensor("dbgc", [128, 4], F32,
                              kind="ExternalOutput").ap()
        dbgs = nc.dram_tensor("dbgs", [128, 32], BF16,
                              kind="ExternalOutput").ap()
        dbgz = nc.dram_tensor("dbgz", [128, 1024], F32,
                              kind="ExternalOutput").ap()

    ctx = {}
    from contextlib import ExitStack
    es = ExitStack()
    with es:
        ARN = es.enter_context(nc.sbuf_tensor("arena", [128, ARENA], BF16))
        pm2 = es.enter_context(nc.sbuf_tensor("pm2", [128, 8 * FF], BF16))
        identb = es.enter_context(nc.sbuf_tensor("identb", [128, 128], BF16))
        identf = es.enter_context(nc.sbuf_tensor("identf", [128, 128], F32))
        onesf = es.enter_context(nc.sbuf_tensor("onesf", [1, 130], F32))
        epsrow = es.enter_context(nc.sbuf_tensor("epsrow", [1, 512], F32))
        nhalf = es.enter_context(nc.sbuf_tensor("nhalf", [128, 2], BF16))
        negstab = es.enter_context(nc.sbuf_tensor("negstab", [1, 16], F32))
        stabrow = es.enter_context(nc.sbuf_tensor("stabrow", [1, 96], F32))
        rowmaxes = es.enter_context(nc.sbuf_tensor("rowmaxes", [128, 16], F32))
        rmts = es.enter_context(nc.sbuf_tensor("rmts", [128, 32], F32))
        stat = es.enter_context(nc.sbuf_tensor("stat", [128, 20], F32))
        negc = es.enter_context(nc.sbuf_tensor("negc", [128, 12], F32))
        iot = es.enter_context(nc.sbuf_tensor("iot", [128, 128], F32))
        piot = es.enter_context(nc.sbuf_tensor("piot", [128, 1], F32))
        ksumc = es.enter_context(nc.sbuf_tensor("ksumc", [128, 32], BF16))
        scr = es.enter_context(nc.sbuf_tensor("scr", [65, 64], BF16))
        stabhl = es.enter_context(nc.sbuf_tensor("stabhl", [1, 160], BF16))
        stabf = es.enter_context(nc.sbuf_tensor("stabf", [1, 16], F32))
        cpool = es.enter_context(nc.sbuf_tensor("cpool", [128, 4], F32))
        psA = es.enter_context(nc.psum_tensor("ps", [128, 4096], F32))
        SEM_NAMES = ("cT", "cV", "cS", "cG",
                     "sLDX", "sLDPM", "sLDSH", "sWG0", "sWG1",
                     "sWD0", "sWD1", "sWD2", "sWD3",
                     "sOR0", "sOR1", "sOUT", "sSP", "sAGW", "sAGK",
                     "sRK", "sST", "sX1", "sXS2", "sX2P0", "sX2P1",
                     "sTPX0", "sTPX1", "sTPH0", "sTPH1", "sTPL", "sTPQ",
                     "sTPK", "sTPW", "sQK0", "sQK1", "sWV", "cc1", "cc2",
                     "sXS20", "sXS21", "sDBG", "sWDC")
        sems = {n: es.enter_context(nc.semaphore(n)) for n in SEM_NAMES}
        block = es.enter_context(nc.Block())
        ar_t = ARN.ap().tensor
        PSTR = ARN.ap().ap[0][0]

        def b16v(col, width, rows=128):
            assert col + width <= ARENA, (col, width)
            return bass.AP(ar_t, col, [[PSTR, rows], [1, width]])

        def f32v(col, width, rows=128):
            assert col % 2 == 0 and col + 2 * width <= ARENA, (col, width)
            return bass.AP(ar_t, col, [[PSTR, rows], [1, 2 * width]]).bitcast(F32)

        def f32bc(col, rows, B):
            ap = f32v(col, B, rows)
            return bass.AP(ap.tensor, ap.offset,
                           [[ap.ap[0][0], rows], [1, B], [0, 16]])

        def bank(i, cols=512, rows=128, off=0):
            return psA[0:rows, 512 * i + off: 512 * i + off + cols]

        cnt = {k: 0 for k in sems}
        _ENG = {"cT", "cV", "cS", "cG"}

        def inc(inst, ch):
            step = 1 if (ch in _ENG or ch.startswith("cc")) else 16
            inst.then_inc(sems[ch], step)
            cnt[ch] += step
            return cnt[ch]

        def wt(eng, ch, val):
            if val > 0:
                eng.wait_ge(sems[ch], val)

        @block.sync
        def _(s):
            xdst = f32v(X0, NT * 1024).rearrange("p (t d) -> p t d", d=1024)
            inc(s.dma_start(out=xdst, in_=x_d[:, :].rearrange(
                "(t p) d -> p t d", p=128)), "sLDX")
        ld_x = cnt["sLDX"]

        @block.sync
        def _(s):
            inc(s.dma_start(out=f32v(PML, 2048, rows=HD), in_=pmt_d[:, :]),
                "sLDPM")
        ld_pm = cnt["sLDPM"]

        nc.const_aps.register(F32, EPS_LN, cpool[:, 0:1])
        nc.const_aps.register(F32, 0.0, cpool[:, 1:2])
        nc.const_aps.register(F32, 1.0, cpool[:, 2:3])

        @block.vector
        def _(v):
            v.memset(cpool[:, 0:1], EPS_LN)
            v.memset(cpool[:, 1:2], 0.0)
            v.memset(cpool[:, 2:3], 1.0)
            v.memset(cpool[:, 3:4], -0.0)
            v.memset(onesf[:, :], 1.0)
            v.memset(epsrow[:, :], EPS_Z)
            v.memset(nhalf[:, :], 0.0)
            v.memset(pm2[:, :], 0.0)
            v.memset(rowmaxes[:, :], -3.0e38)
            v.drain()
            v.memset(nhalf[0:64, 0:1], -0.5)
            inc(v.memset(nhalf[64:128, 1:2], -0.5), "cV")
        consts_v = cnt["cV"]

        @block.gpsimd
        def _(g):
            g.iota(iot[:, :], [[1, 128]], channel_multiplier=0,
                   allow_small_or_imprecise_dtypes=True)
            inc(g.iota(piot[:, :], [[0, 1]], channel_multiplier=1,
                       allow_small_or_imprecise_dtypes=True), "cG")
        iota_g = cnt["cG"]

        @block.vector
        def _(v):
            wt(v, "cG", iota_g)
            v.tensor_single_scalar(identf[:, :], iot[:, :], piot[:, 0:1],
                                   ALU.is_equal)
            v.drain()
            inc(v.tensor_copy(identb[:, :], identf[:, :]), "cV")
        ident_v = cnt["cV"]

        @block.scalar
        def _(sc):
            wt(sc, "sLDPM", ld_pm)
            wt(sc, "cV", consts_v)
            pml = f32v(PML, 2048, rows=HD)
            for p in range(8):
                sc.mul(pm2[0:HD, p * FF: p * FF + F2],
                       pml[:, (2 * p) * F2:(2 * p) * F2 + F2], 0.125)
                last = sc.mul(pm2[64:64 + HD, p * FF + F2:(p + 1) * FF],
                              pml[:, (2 * p + 1) * F2:(2 * p + 1) * F2 + F2],
                              0.125)
            inc(last, "cS")
        pm2_s = cnt["cS"]

        qdq_prev = {0: 0, 1: 0}

        def emit_qdq(src, dst, W, rows=128, rot=0, waits=()):
            base = QDQ_SC + rot * 4480
            um = f32v(base, W, rows)
            am = f32v(base + 2048, W, rows)
            mcol, icol = base + 4096, base + 4224
            B = W // 16
            mm_ = f32v(mcol, B, rows)
            im_ = f32v(icol, B, rows)
            sm = f32v(QSM + rot * 2048, W, rows)
            src3 = src.rearrange("p (b c) -> p b c", c=16)
            prev_g = qdq_prev[rot]
            I32 = mybir.dt.int32

            @block.vector
            def _(v):
                wt(v, "cG", prev_g)
                for ch, val in waits:
                    wt(v, ch, val)
                v.drain()
                v.tensor_single_scalar(sm.bitcast(I32), src.bitcast(I32),
                                       cpool[0:rows, 3:4].bitcast(I32),
                                       ALU.bitwise_and)
                v.tensor_reduce(mm_, src3, AX.X, ALU.max,
                                apply_absolute_value=True)
                v.drain()
                v.tensor_scalar(mm_, mm_, 1.0 / 12.0, 0.5e-12, ALU.mult,
                                ALU.max)
                v.drain()
                v.reciprocal(im_, mm_)
                v.drain()
                inc(v.tensor_scalar_mul(im_, im_, 2.0), "cV")
            minv_v = cnt["cV"]

            @block.gpsimd
            def _(g):
                wt(g, "cV", minv_v)
                g.tensor_tensor(
                    um.rearrange("p (b c) -> p b c", c=16), src3,
                    f32bc(icol, rows, B), ALU.mult)
                g.drain()
                inc(g.tensor_tensor(um, um, um, ALU.mult), "cG")
            ns_g = cnt["cG"]

            @block.vector
            def _(v):
                wt(v, "cG", ns_g)
                v.drain()
                v._custom_dve(QDQ_A, out=am, in0=um, s0=9.0, s1=25.0,
                              imm2=49.0)
                v.drain()
                with nc.allow_low_precision(reason="T ladder ints exact"):
                    v._custom_dve(QDQ_T, out=dst, in0=um, in1=am, s0=100.0,
                                  s1=196.0, imm2=400.0)
                v.drain()
                inc(v.tensor_tensor(sm.bitcast(I32), sm.bitcast(I32),
                                    f32bc(mcol, rows, B).bitcast(I32),
                                    ALU.bitwise_xor), "cV")
            lad_v = cnt["cV"]

            @block.gpsimd
            def _(g):
                wt(g, "cV", lad_v)
                g.drain()
                inc(g.tensor_tensor(dst, dst, sm, ALU.mult), "cG")
            qdq_prev[rot] = cnt["cG"]
            return cnt["cG"]

        tp_prev = 0

        def prep_gateup(w_hbm, sec_off, tp_prev):
            ld = []

            @block.sync
            def _(s):
                for j, (r0, P) in enumerate(((0, 128), (128, 128), (256, 96))):
                    inc(s.dma_start(
                        out=f32v(WSH + j * 2048, 1024, rows=P),
                        in_=w_hbm[r0:r0 + P, :]), "sLDSH")
            ld_total = cnt["sLDSH"]
            for j, (r0, P) in enumerate(((0, 128), (128, 128), (256, 96))):
                src = f32v(WSH + j * 2048, 1024, rows=P)
                q = emit_qdq(
                    src, bass.AP(ar_t, WSHQ, [[PSTR, P], [1, 1024]]),
                    1024, rows=P, rot=j % 2,
                    waits=(("sLDSH", ld_total), ("sTPW", tp_prev)))

                @block.scalar
                def _(sc):
                    wt(sc, "cG", q)
                    wt(sc, "sAGW", cnt["sAGW"])
                    src_t = bass.AP(ar_t, WSHQ, [[PSTR, P], [128, 8], [1, 128]])
                    dst_t = bass.AP(ar_t, WSHT + j * 128,
                                    [[PSTR, 128], [ISH, 8], [1, P]])
                    inc(sc.dma_start_transpose(dst_t, src_t),
                        "sTPW")
                tp_prev = cnt["sTPW"]

            @block.sync
            def _(s):
                wt(s, "sTPW", tp_prev)
                inc(s.dma_start(out=agw_in.ap()[:, sec_off:sec_off + WPART],
                                in_=b16v(WSHT, WPART)), "sAGW")
            return tp_prev

        tp_prev = prep_gateup(wg_d, 0, tp_prev)
        tp_prev = prep_gateup(wu_d, WPART, tp_prev)

        @block.sync
        def _(s):
            wt(s, "cG", qdq_prev[0])
            wt(s, "cG", qdq_prev[1])
            inc(s.dma_start(out=f32v(WSH, 2816), in_=wd_d[:, :]), "sLDSH")
        ld_wd = cnt["sLDSH"]
        for ci, (cpos, cw) in enumerate(((0, 1024), (1024, 1024),
                                         (2048, 768))):
            q = emit_qdq(f32v(WSH, 2816)[:, cpos:cpos + cw],
                         b16v(WSHQ, 2816)[:, cpos:cpos + cw], cw,
                         rot=ci % 2,
                         waits=(("sLDSH", ld_wd), ("sTPW", tp_prev)))

        @block.scalar
        def _(sc):
            wt(sc, "cG", q)
            wt(sc, "sAGW", cnt["sAGW"])
            src_t = bass.AP(ar_t, WSHQ, [[PSTR, 128], [128, 22], [1, 128]])
            dst_t = bass.AP(ar_t, WSHT, [[PSTR, 128], [128, 22], [1, 128]])
            inc(sc.dma_start_transpose(dst_t, src_t), "sTPW")
        wd_tp = cnt["sTPW"]

        @block.sync
        def _(s):
            wt(s, "sTPW", wd_tp)
            inc(s.dma_start(out=agw_in.ap()[:, 2 * WPART:3 * WPART],
                            in_=b16v(WSHT, WPART)), "sAGW")
        st_d = cnt["sAGW"]

        @block.gpsimd
        def _(g):
            wt(g, "sAGW", st_d)
            inc(g.collective_compute(
                "AllGather", ALU.bypass, replica_groups=[list(range(NCORE))],
                ins=[agw_in.ap().opt()], outs=[agw_out.ap().opt()]), "cc1")
        agw_done = cnt["cc1"]

        @block.sync
        def _(s):
            wt(s, "cc1", agw_done)
            ag_t = agw_out.ap().tensor
            for r in range(8):
                hb2, rr = r // 4, r % 4
                src = bass.AP(ag_t, r * 128 * WRANK + 2 * WPART,
                              [[WRANK, 128], [128, 22], [1, 128]])
                dst = bass.AP(wdc.ap().tensor, hb2 * 22 * 128 * 512 + rr * 128,
                              [[512, 128], [128 * 512, 22], [1, 128]])
                inc(s.dma_start(out=dst, in_=src), "sWDC")
        wdc_done = cnt["sWDC"]

        st8 = {"resid_t": 0, "resid_v": 0}

        def emit_ffn(kind):
            x_dram = (kind == 3)
            wslot_t = {0: 0, 1: 0}
            wdslot_t = {}
            bankev = {0: 0, 1: 0}
            silu_used = {0: 0, 1: 0}
            hused = {}
            lnst_prev = 0
            down_prev_t = 0
            outdma = {0: 0, 1: 0}
            hq_tp = {0: 0, 1: 0, 2: 0, 3: 0}
            def prep_tiles(tiles, xq_tp):
                nonlocal lnst_prev
                for jt, t in enumerate(tiles):
                    rot = jt % 2
                    if x_dram:
                        xt = f32v(X0 + rot * 2048, 1024)
                        xch = "sX2P%d" % rot

                        @block.sync
                        def _(s):
                            wt(s, "cV", lnst_prev)
                            wt(s, "cV", st8["resid_v"])
                            inc(s.dma_start(
                                out=xt,
                                in_=xsp2.ap()[t * 128:(t + 1) * 128, :]), xch)
                        x_wait = (xch, cnt[xch])
                    else:
                        xt = f32v(X0 + t * 2048, 1024)
                        x_wait = ("sLDX", ld_x)
                    lnr = f32v(LNR + rot * 2048, 1024)

                    @block.vector
                    def _(v):
                        wt(v, x_wait[0], x_wait[1])
                        wt(v, "cT", st8["resid_t"])
                        v.drain()
                        v.bn_stats(stat[:, 0:6], xt[:, 0:512])
                        v.bn_stats(stat[:, 6:12], xt[:, 512:1024])
                        v.drain()
                        inc(v.bn_aggr(stat[:, 12:14], stat[:, 0:12]), "cV")
                    lnst_prev = cnt["cV"]

                    @block.scalar
                    def _(sc):
                        wt(sc, "cV", lnst_prev)
                        inc(sc.activation(stat[:, 14:15], stat[:, 13:14],
                                          AF.Sqrt, bias=EPS_LN), "cS")
                    sq_s = cnt["cS"]

                    @block.vector
                    def _(v):
                        wt(v, "cS", sq_s)
                        v.drain()
                        v.reciprocal(stat[:, 15:16], stat[:, 14:15])
                        v.drain()
                        inc(v.tensor_scalar(lnr, xt, stat[:, 12:13],
                                            stat[:, 15:16], ALU.subtract,
                                            ALU.mult), "cV")
                    ln_v = cnt["cV"]

                    xqt = b16v(XQR + rot * 1024, 1024)
                    tpch = "sTPX%d" % rot
                    q = emit_qdq(lnr, xqt, 1024, rot=rot,
                                 waits=(("cV", ln_v), (tpch, xq_tp[rot])))

                    @block.scalar
                    def _(sc):
                        wt(sc, "cG", q)
                        wt(sc, "cT", wslot_t[0])
                        wt(sc, "cT", wslot_t[1])
                        src_t = bass.AP(ar_t, XQR + rot * 1024,
                                        [[PSTR, 128], [128, 8], [1, 128]])
                        dst_t = bass.AP(ar_t, XQT + jt * 128,
                                        [[PSTR, 128], [NTQ * 128, 8], [1, 128]])
                        inc(sc.dma_start_transpose(dst_t, src_t),
                            tpch)
                    xq_tp[rot] = cnt[tpch]

            xq_tp_q = {0: {0: 0, 1: 0}}
            prep_tiles(list(range(NTQ)), xq_tp_q[0])
            for tq in range(NQ):
                tiles = list(range(tq * NTQ, (tq + 1) * NTQ))
                xq_tp = xq_tp_q[tq]

                for r in range(8):
                    sl = r % 2

                    wch = "sWG%d" % sl

                    @block.sync
                    def _(s):
                        wt(s, "cc1", agw_done)
                        wt(s, "cT", wslot_t[sl])
                        inc(s.dma_start(out=b16v(WBUF + sl * 2816, 2816),
                                        in_=agw_out.ap()[r, :, 0:WPART]), wch)
                        inc(s.dma_start(
                            out=b16v(WBUF + (2 + sl) * 2816, 2816),
                            in_=agw_out.ap()[r, :, WPART:2 * WPART]), wch)
                    wld = cnt[wch]
                    for jt, t in enumerate(tiles):
                        bA = (jt % 2) * 2
                        bB = bA + 1
                        hcol = (HR if jt < 2 else HR2) + (jt % 2) * 1408

                        @block.tensor
                        def _(tt):
                            wt(tt, wch, wld)
                            wt(tt, "sTPX0", xq_tp[0])
                            wt(tt, "sTPX1", xq_tp[1])
                            wt(tt, "cV", bankev[jt % 2])
                            for k in range(8):
                                lhs = b16v(XQT + k * NTQ * 128 + jt * 128, 128)
                                tt.ldweights(lhs)
                                tt.matmul(bank(bA, 352), lhs,
                                          b16v(WBUF + sl * 2816 + k * 352, 352),
                                          start=(k == 0), stop=(k == 7))
                                last = tt.matmul(
                                    bank(bB, 352), lhs,
                                    b16v(WBUF + (2 + sl) * 2816 + k * 352, 352),
                                    start=(k == 0), stop=(k == 7))
                            inc(last, "cT")
                        wslot_t[sl] = cnt["cT"]
                        mm_t = cnt["cT"]

                        @block.scalar
                        def _(sc):
                            wt(sc, "cT", mm_t)
                            wt(sc, "cV", silu_used[sl])
                            inc(sc.activation(f32v(SILU + sl * 704, 352),
                                              bank(bA, 352), AF.Silu), "cS")
                        si_s = cnt["cS"]

                        @block.vector
                        def _(v):
                            wt(v, "cS", si_s)
                            wt(v, "cG", hused.get((jt, r % 2), 0))
                            v.drain()
                            hb = f32v(hcol, 704)
                            inc(v.tensor_tensor(
                                hb[:, (r % 2) * 352:(r % 2) * 352 + 352],
                                bank(bB, 352), f32v(SILU + sl * 704, 352),
                                ALU.mult), "cV")
                        bankev[jt % 2] = cnt["cV"]
                        silu_used[sl] = cnt["cV"]
                        hev = cnt["cV"]

                        if r % 2 == 1:
                            hb = f32v(hcol, 704)
                            hqcol = ((HQR if jt < 2 else HQ23)
                                     + (jt % 2) * 2816)
                            hq = b16v(hqcol, 2816)
                            q = emit_qdq(hb, hq[:, (r - 1) * 352:(r + 1) * 352],
                                         704, rot=jt % 2,
                                         waits=(("cV", hev),
                                                ("sTPH%d" % (jt % 2),
                                                 hq_tp[jt])))
                            hused[(jt, 0)] = cnt["cG"]
                            hused[(jt, 1)] = cnt["cG"]
                            if r == 7:
                                @block.scalar
                                def _(sc):
                                    wt(sc, "cG", q)
                                    wt(sc, "cT", down_prev_t)
                                    src_t = bass.AP(
                                        ar_t, hqcol,
                                        [[PSTR, 128], [128, 22], [1, 128]])
                                    dst_t = bass.AP(
                                        ar_t, HQT + jt * 128,
                                        [[PSTR, 128], [NTQ * 128, 22], [1, 128]])
                                    inc(sc.dma_start_transpose(
                                        dst_t, src_t),
                                        "sTPH%d" % (jt % 2))
                                hq_tp[jt] = cnt["sTPH%d" % (jt % 2)]

                if tq + 1 < NQ:
                    xq_tp_q[tq + 1] = {0: xq_tp[0], 1: xq_tp[1]}
                    prep_tiles(list(range((tq + 1) * NTQ, (tq + 2) * NTQ)),
                               xq_tp_q[tq + 1])

                for hb2 in range(2):
                    for ic in range(22):
                        wslot = ic % 4
                        wdch = "sWD%d" % wslot

                        @block.sync
                        def _(s):
                            wt(s, "sWDC", wdc_done)
                            wt(s, "cT", wdslot_t.get(wslot, 0))
                            src = bass.AP(
                                wdc.ap().tensor,
                                (hb2 * 22 + ic) * 128 * 512,
                                [[512, 128], [1, 512]])
                            inc(s.dma_start(out=b16v(WDB + wslot * 512, 512),
                                            in_=src), wdch)
                        wdld = cnt[wdch]

                        @block.tensor
                        def _(tt):
                            wt(tt, wdch, wdld)
                            wt(tt, "sTPH0", max(hq_tp[0], hq_tp[2]))
                            wt(tt, "sTPH1", max(hq_tp[1], hq_tp[3]))
                            if ic == 0:
                                wt(tt, "cV", st8["resid_v"])
                            for jt in range(NTQ):
                                lhs = b16v(HQT + ic * NTQ * 128 + jt * 128, 128)
                                tt.ldweights(lhs)
                                last = tt.matmul(
                                    bank(4 + jt), lhs,
                                    b16v(WDB + wslot * 512, 512),
                                    start=(ic == 0), stop=(ic == 21))
                            inc(last, "cT")
                        wdslot_t[wslot] = cnt["cT"]
                    dn_t = cnt["cT"]
                    down_prev_t = dn_t

                    for jt, t in enumerate(tiles):
                        if kind == 1:
                            @block.vector
                            def _(v):
                                wt(v, "cT", dn_t)
                                xsl = f32v(X0 + t * 2048 + hb2 * 1024, 512)
                                inc(v.scalar_tensor_tensor(
                                    xsl, bank(4 + jt), 0.5, xsl, ALU.mult,
                                    ALU.add), "cV")
                            st8["resid_v"] = cnt["cV"]
                        else:
                            orch = "sOR%d" % (t % 2)

                            @block.sync
                            def _(s):
                                wt(s, "cV", st8["resid_v"])
                                wt(s, "sOUT", cnt["sOUT"])
                                inc(s.dma_start(
                                    out=f32v(OUTR + (t % 2) * 1024, 512),
                                    in_=xsp2.ap()[t * 128:(t + 1) * 128,
                                                  hb2 * 512:(hb2 + 1) * 512]),
                                    orch)
                            x2l = cnt[orch]

                            @block.vector
                            def _(v):
                                wt(v, "cT", dn_t)
                                wt(v, orch, x2l)
                                orow = f32v(OUTR + (t % 2) * 1024, 512)
                                inc(v.scalar_tensor_tensor(
                                    orow, bank(4 + jt), 0.5, orow, ALU.mult,
                                    ALU.add), "cV")
                            st8["resid_v"] = cnt["cV"]
                            ev_v = cnt["cV"]

                            @block.sync
                            def _(s):
                                wt(s, "cV", ev_v)
                                inc(s.dma_start(
                                    out=out_d[t * 128:(t + 1) * 128,
                                              hb2 * 512:(hb2 + 1) * 512],
                                    in_=f32v(OUTR + (t % 2) * 1024, 512)),
                                    "sOUT")
                            outdma[t % 2] = cnt["sOUT"]
                    if kind == 1:
                        st8["resid_t"] = 0
            return

        emit_ffn(1)
        ffn1_v = cnt["cV"]

        if stop_after == "ffn1":
            _epilogue(block, wt, cnt)
            return nc

        lnst_prev = 0
        for t in range(NT):
            xt = f32v(X0 + t * 2048, 1024)

            @block.sync
            def _(s):
                wt(s, "cV", ffn1_v)
                inc(s.dma_start(out=xsp1.ap()[t * 128:(t + 1) * 128, :],
                                in_=xt), "sSP")

            @block.vector
            def _(v):
                wt(v, "cV", ffn1_v)
                v.drain()
                v.bn_stats(stat[:, 0:6], xt[:, 0:512])
                v.bn_stats(stat[:, 6:12], xt[:, 512:1024])
                v.drain()
                inc(v.bn_aggr(stat[:, 12:14], stat[:, 0:12]), "cV")
            lnst_prev = cnt["cV"]

            @block.scalar
            def _(sc):
                wt(sc, "cV", lnst_prev)
                inc(sc.activation(stat[:, 14:15], stat[:, 13:14], AF.Sqrt,
                                  bias=EPS_LN), "cS")
            sq_s = cnt["cS"]

            @block.vector
            def _(v):
                wt(v, "cS", sq_s)
                v.drain()
                v.reciprocal(stat[:, 15:16], stat[:, 14:15])
                v.drain()
                with nc.allow_low_precision(reason="ln2 bf16 rows (as twin)"):
                    inc(v.tensor_scalar(b16v(A_LN2B + t * 1024, 1024), xt,
                                        stat[:, 12:13], stat[:, 15:16],
                                        ALU.subtract, ALU.mult), "cV")
            ln_v = cnt["cV"]
        ln2b_v = cnt["cV"]

        @block.scalar
        def _(sc):
            wt(sc, "cV", ln2b_v)
            wt(sc, "sSP", cnt["sSP"])
            for t in range(NT):
                src_t = bass.AP(ar_t, A_LN2B + t * 1024,
                                [[PSTR, 128], [128, 8], [1, 128]])
                dst_t = bass.AP(ar_t, A_LN2T + t * 128,
                                [[PSTR, 128], [TOK, 8], [1, 128]])
                inc(sc.dma_start_transpose(dst_t, src_t), "sTPL")
        ln2T_tp = cnt["sTPL"]

        @block.gpsimd
        def _(g):
            wt(g, "cV", ln2b_v)
            inc(g.dma_start(
                out=b16v(A_WV, 8192).rearrange("p (k c) -> p k c", k=8),
                in_=wqkv_d[:, 2048:3072].rearrange("(k p) c -> p k c", p=128)),
                "sWV")
        wv_ld = cnt["sWV"]

        qkev = {0: 0, 1: 0}
        wqk_used = {0: 0, 1: 0}
        for ck in range(16):
            sl = ck % 2

            qch = "sQK%d" % sl

            @block.gpsimd
            def _(g):
                wt(g, "cT", wqk_used[sl])
                inc(g.dma_start(
                    out=b16v(A_WQK + sl * 1024, 1024).rearrange(
                        "p (k c) -> p k c", k=8),
                    in_=wqkv_d[:, ck * 128:(ck + 1) * 128].rearrange(
                        "(k p) c -> p k c", p=128)), qch)
            wqk_ld = cnt[qch]

            @block.tensor
            def _(tt):
                wt(tt, qch, wqk_ld)
                wt(tt, "sTPL", ln2T_tp)
                wt(tt, "cV", qkev[sl])
                grp = sl * 4
                for k in range(8):
                    lhs = b16v(A_WQK + sl * 1024 + k * 128, 128)
                    tt.ldweights(lhs)
                    for tb in range(TB):
                        last = tt.matmul(bank(grp + tb), lhs,
                                         b16v(A_LN2T + k * TOK + tb * 512, 512),
                                         start=(k == 0), stop=(k == 7))
                inc(last, "cT")
            wqk_used[sl] = cnt["cT"]
            qk_t = cnt["cT"]

            @block.vector
            def _(v):
                wt(v, "cT", qk_t)
                grp = sl * 4
                base = (A_Q + ck * TOK) if ck < 8 else (A_K + (ck - 8) * TOK)
                for tb in range(TB):
                    last = v.tensor_copy(b16v(base + tb * 512, 512),
                                         bank(grp + tb))
                inc(last, "cV")
            qkev[sl] = cnt["cV"]
        qk_done_v = cnt["cV"]

        @block.vector
        def _(v):
            wt(v, "cV", ln2b_v)
            v.memset(b16v(A_KVP, NH * 264, rows=65), 0.0)
            v.drain()
            v.memset(b16v(A_V65, NT * 1040), 0.0)
            v.drain()
            ones_ap = bass.AP(ar_t, A_V65 + 1024,
                              [[PSTR, 128], [1040, NT], [65, 16]])
            inc(v.memset(ones_ap, 1.0), "cV")
        v65i = cnt["cV"]

        vev = dict(qkev)
        for ts in range(NT):
            @block.tensor
            def _(tt):
                wt(tt, "sWV", wv_ld)
                wt(tt, "sTPL", ln2T_tp)
                wt(tt, "cV", vev[ts % 2])
                for k in range(8):
                    lhs = b16v(A_LN2T + k * TOK + ts * 128, 128)
                    tt.ldweights(lhs)
                    for vb in range(2):
                        last = tt.matmul(bank((ts % 2) * 2 + vb), lhs,
                                         b16v(A_WV + k * 1024 + vb * 512, 512),
                                         start=(k == 0), stop=(k == 7))
                inc(last, "cT")
            v_t = cnt["cT"]

            @block.vector
            def _(v):
                wt(v, "cT", v_t)
                wt(v, "cV", v65i)
                for vb in range(2):
                    dst = bass.AP(ar_t, A_V65 + ts * 1040 + vb * 8 * 65,
                                  [[PSTR, 128], [65, 8], [1, 64]])
                    last = v.tensor_copy(dst, bank((ts % 2) * 2 + vb).rearrange(
                        "p (h d) -> p h d", d=64))
                inc(last, "cV")
            vev[ts % 2] = cnt["cV"]
        v_done_v = cnt["cV"]

        upar = {6: v_done_v, 7: v_done_v}
        for p in range(8):
            for ts in range(NT):
                ub = 6 + (ts % 2)

                @block.tensor
                def _(tt):
                    wt(tt, "cS", pm2_s)
                    wt(tt, "cV", qk_done_v)
                    wt(tt, "cV", upar[ub])
                    lhs = b16v(A_K + p * TOK + ts * 128, 128)
                    tt.ldweights(lhs)
                    inc(tt.matmul(bank(ub, 256), lhs,
                                  pm2[:, p * FF:(p + 1) * FF],
                                  start=True, stop=True), "cT")
                u_t = cnt["cT"]

                @block.vector
                def _(v):
                    wt(v, "cT", u_t)
                    v.drain()
                    inc(v.tensor_reduce(
                        rmts[:, 2 * ts:2 * ts + 2], bank(ub, 256).rearrange(
                            "p (h f) -> p h f", f=128), AX.X, ALU.max), "cV")
                upar[ub] = cnt["cV"]

            @block.vector
            def _(v):
                v.drain()
                src = bass.AP(rmts.ap().tensor, 0, [[32, 128], [1, 2], [2, NT]])
                inc(v.tensor_reduce(rowmaxes[:, 2 * p:2 * p + 2], src,
                                    AX.X, ALU.max), "cV")
        rmax_v = cnt["cV"]

        @block.tensor
        def _(tt):
            wt(tt, "cV", rmax_v)
            wt(tt, "cV", ident_v)
            inc(tt.transpose(psA[0:16, 3072:3200], rowmaxes[:, :],
                             identf[:, :]), "cT")
        rtp_t = cnt["cT"]

        @block.vector
        def _(v):
            wt(v, "cT", rtp_t)
            v.tensor_reduce(stat[0:16, 16:17], psA[0:16, 3072:3200], AX.X,
                            ALU.max)
            v.drain()
            inc(v.tensor_scalar_mul(stat[0:16, 17:18], stat[0:16, 16:17],
                                    -1.0), "cV")
        nstc_v = cnt["cV"]

        @block.tensor
        def _(tt):
            wt(tt, "cV", nstc_v)
            inc(tt.transpose(psA[0:1, 3072:3088], stat[0:16, 17:18],
                             identf[0:16, 0:16]), "cT")
        nst_t = cnt["cT"]

        @block.vector
        def _(v):
            wt(v, "cT", nst_t)
            v.tensor_copy(negstab[0:1, 0:16], psA[0:1, 3072:3088])
            v.drain()
            with nc.allow_low_precision(reason="hi/lo bf16 pair is exact"):
                v.tensor_copy(stabhl[0:1, 0:16], negstab[0:1, 0:16])
            v.drain()
            v.tensor_copy(stabf[0:1, :], stabhl[0:1, 0:16])
            v.drain()
            with nc.allow_low_precision(reason="hi/lo bf16 pair is exact"):
                inc(v.tensor_tensor(stabhl[0:1, 16:32], negstab[0:1, 0:16],
                                    stabf[0:1, :], ALU.subtract), "cV")
        negstab_v = cnt["cV"]

        KF4 = A_LN2T
        AGB = 24576
        args_done = {0: 0, 1: 0, 2: 0, 3: 0}
        argb_used = {6: 0, 7: 0}
        kf_used = {0: 0, 1: 0, 2: 0, 3: 0}
        kv_read_v = 0
        q2_used = 0
        for p in range(8):
            @block.vector
            def _(v):
                wt(v, "cV", v_done_v)
                wt(v, "cT", q2_used)
                v.drain()
                inc(v.tensor_tensor(b16v(A_Q2, TOK), b16v(A_K + p * TOK, TOK),
                                    b16v(A_K + p * TOK, TOK), ALU.mult), "cV")
            k2_s = cnt["cV"]

            st_c2 = {}

            def emit_u2(ts, p=p, k2_s=k2_s):
                ub = ts % 4
                bb = 6 + (ts % 2)

                @block.tensor
                def _(tt):
                    wt(tt, "cV", k2_s)
                    wt(tt, "cV", args_done[ub])
                    wt(tt, "cV", negstab_v)
                    lhs = b16v(A_K + p * TOK + ts * 128, 128)
                    tt.ldweights(lhs)
                    tt.matmul(bank(ub, 256), lhs, pm2[:, p * FF:(p + 1) * FF],
                              start=True, stop=False)
                    lhs2 = b16v(A_Q2 + ts * 128, 128)
                    tt.ldweights(lhs2)
                    tt.matmul(bank(ub, 2, off=256), lhs2, nhalf[:, :],
                              start=False, stop=False)
                    inc(tt.matmul(bank(ub, 2, off=256), onesf[0:1, 0:128],
                                  negstab[0:1, 2 * p:2 * p + 2],
                                  start=False, stop=True), "cT")
                st_c2[("u2", ts)] = cnt["cT"]

                agb = AGB + (bb - 6) * 1024

                @block.vector
                def _(v):
                    wt(v, "cT", st_c2[("u2", ts)])
                    wt(v, "cS", argb_used[bb])
                    v.drain()
                    for hh in range(2):
                        nd = bank(ub, 1, off=256 + hh)
                        v.tensor_scalar(f32v(agb + hh * 512, 128),
                                        bank(ub, 128, off=hh * 128), nd,
                                        None, ALU.add)
                        v.tensor_scalar(f32v(agb + hh * 512 + 256, 128),
                                        bank(ub, 128, off=hh * 128), nd,
                                        -1.0, ALU.subtract, ALU.mult)
                    inc(v.drain(), "cV")
                args_done[ub] = cnt["cV"]
                st_c2[("args", ts)] = cnt["cV"]

                @block.scalar
                def _(sc):
                    wt(sc, "cV", st_c2[("args", ts)])
                    wt(sc, "cT", kf_used[ub])
                    inc(sc.activation(b16v(KF4 + ub * 512, 512),
                                      f32v(agb, 512), AF.Exp), "cS")
                argb_used[bb] = cnt["cS"]
                st_c2[("kf", ts)] = cnt["cS"]

            def emit_kv(ts, p=p):
                ub = ts % 4

                @block.tensor
                def _(tt):
                    wt(tt, "cS", st_c2[("kf", ts)])
                    if ts == 0:
                        wt(tt, "cV", kv_read_v)
                    for hh in range(2):
                        lhs = bass.AP(ar_t,
                                      A_V65 + ts * 1040 + (2 * p + hh) * 65,
                                      [[PSTR, 128], [1, 65]])
                        tt.ldweights(lhs)
                        last = tt.matmul(bank(4 + hh, 256, rows=65), lhs,
                                         b16v(KF4 + ub * 512 + hh * 256, 256),
                                         start=(ts == 0), stop=(ts == NT - 1))
                    inc(last, "cT")
                kf_used[ub] = cnt["cT"]

            for ts in range(NT):
                emit_u2(ts)
                if ts >= 3:
                    emit_kv(ts - 3)
            for ts in range(NT - 3, NT):
                emit_kv(ts)
            q2_used = cnt["cT"]
            kv_t = cnt["cT"]

            @block.vector
            def _(v):
                wt(v, "cT", kv_t)
                for hh in range(2):
                    dst = bass.AP(ar_t, A_KVP + (2 * p + hh) * 264,
                                  [[PSTR, 65], [1, 256]])
                    last = v.tensor_copy(dst, bank(4 + hh, 256, rows=65))
                inc(last, "cV")
            kv_read_v = cnt["cV"]
        kvp_v = cnt["cV"]

        @block.sync
        def _(s):
            wt(s, "cV", kvp_v)
            inc(s.dma_start(out=agk_in.ap()[0:1, 0:KVE].rearrange(
                "a (p w) -> (a p) w", p=65),
                in_=b16v(A_KVP, NH * 264, rows=65)), "sAGK")
            inc(s.dma_start(
                out=agk_in.ap()[0:1, KVE:KVE + 32],
                in_=stabhl[0:1, 0:32]), "sAGK")
        agks = cnt["sAGK"]

        @block.gpsimd
        def _(g):
            wt(g, "sAGK", agks)
            inc(g.collective_compute(
                "AllGather", ALU.bypass,
                replica_groups=[[0, 1, 2, 3], [4, 5, 6, 7]],
                ins=[agk_in.ap().opt()], outs=[agk_out.ap().opt()]), "cc2")
        agk_done = cnt["cc2"]

        @block.sync
        def _(s):
            wt(s, "cc2", agk_done)
            src = bass.AP(agk_out.ap().tensor, agk_out.ap().offset + KVE,
                          [[KVE + 32, NGRP], [1, 32]])
            inc(s.dma_start(out=stabhl[0:1, 32:160], in_=src), "sST")
        stld = cnt["sST"]

        @block.vector
        def _(v):
            wt(v, "sST", stld)
            v.drain()
            hi = bass.AP(stabhl.ap().tensor, 32, [[160, 1], [32, 4], [1, 16]])
            lo = bass.AP(stabhl.ap().tensor, 48, [[160, 1], [32, 4], [1, 16]])
            v.tensor_tensor(stabrow[0:1, 0:64].rearrange(
                "a (r h) -> a r h", h=16), hi, lo, ALU.add)
            v.drain()
            vsrc = bass.AP(stabrow.ap().tensor, 0, [[96, 1], [1, 16], [16, 4]])
            v.tensor_reduce(stabrow[0:1, 64:80], vsrc, AX.X, ALU.min)
            v.drain()
            gb = bass.AP(stabrow.ap().tensor, 64, [[96, 1], [0, 4], [1, 16]])
            dv = stabrow[0:1, 0:64]
            inc(v.tensor_tensor(dv, gb, dv, ALU.subtract), "cV")
        stsc_v = cnt["cV"]

        @block.scalar
        def _(sc):
            wt(sc, "cV", stsc_v)
            inc(sc.activation(stabrow[0:1, 0:64], stabrow[0:1, 0:64], AF.Exp),
                "cS")
        scexp_s = cnt["cS"]

        @block.tensor
        def _(tt):
            wt(tt, "cS", scexp_s)
            inc(tt.matmul(psA[0:65, 3072:3136], onesf[0:1, 0:65],
                          stabrow[0:1, 0:64], start=True, stop=True), "cT")
        scr_t = cnt["cT"]

        @block.vector
        def _(v):
            wt(v, "cT", scr_t)
            inc(v.tensor_copy(scr[:, :], psA[0:65, 3072:3136]), "cV")
        scr_v = cnt["cV"]

        cmb_v = 0
        for r in range(NGRP):
            @block.sync
            def _(s):
                wt(s, "cc2", agk_done)
                wt(s, "cV", cmb_v)
                inc(s.dma_start(
                    out=b16v(A_RANK, NH * 264, rows=65),
                    in_=agk_out.ap()[r, 0:KVE].rearrange("(p w) -> p w", p=65)),
                    "sRK")
            rkld = cnt["sRK"]

            @block.vector
            def _(v):
                wt(v, "sRK", rkld)
                wt(v, "cV", scr_v)
                v.drain()
                scb = bass.AP(scr.ap().tensor, r * 16,
                              [[64, 65], [1, 16], [0, 264]])
                rk3 = bass.AP(ar_t, A_RANK, [[PSTR, 65], [264, 16], [1, 264]])
                v.tensor_tensor(rk3, rk3, scb, ALU.mult)
                v.drain()
                if r == 0:
                    last = v.tensor_copy(b16v(A_KVACC, NH * 264, rows=65),
                                         b16v(A_RANK, NH * 264, rows=65))
                else:
                    last = v.tensor_tensor(b16v(A_KVACC, NH * 264, rows=65),
                                           b16v(A_KVACC, NH * 264, rows=65),
                                           b16v(A_RANK, NH * 264, rows=65),
                                           ALU.add)
                inc(last, "cV")
            cmb_v = cnt["cV"]

        @block.vector
        def _(v):
            wt(v, "cV", cmb_v)
            wt(v, "cV", v_done_v)
            v.drain()
            v.memset(bass.AP(ar_t, A_KVGB, [[PSTR, 80], [1, 4096]]), 0.0)
            v.drain()
            dst = bass.AP(ar_t, A_KVGB, [[PSTR, 65], [256, 16], [1, 256]])
            src = bass.AP(ar_t, A_KVACC, [[PSTR, 65], [264, 16], [1, 256]])
            inc(v.tensor_copy(dst, src), "cV")
        kvgb_v = cnt["cV"]

        @block.vector
        def _(v):
            v.memset(b16v(A_KVT2, 4096), 0.0)
            inc(v.drain(), "cV")
        kvt2_z = cnt["cV"]

        KSTG = A_QF
        @block.scalar
        def _(sc):
            wt(sc, "cV", kvgb_v)
            wt(sc, "cT", kf_used[0])
            wt(sc, "cT", kf_used[1])
            src_t = bass.AP(ar_t, A_KVGB,
                            [[PSTR, 80], [128, 32], [1, 128]])
            dst_t = bass.AP(ar_t, KSTG,
                            [[PSTR, 128], [80, 32], [1, 80]])
            inc(sc.dma_start_transpose(dst_t, src_t), "sTPK")
        kstg_tp = cnt["sTPK"]

        @block.vector
        def _(v):
            wt(v, "sTPK", kstg_tp)
            wt(v, "cV", kvt2_z)
            v.drain()
            for h in range(16):
                pair, hh = h // 2, h % 2
                for fc in range(2):
                    sblk = KSTG + (h * 2 + fc) * 80
                    dcol = A_KVT2 + pair * 512 + (hh * 2 + fc) * 128 + hh * 64
                    last = v.tensor_copy(
                        bass.AP(ar_t, dcol, [[PSTR, 128], [1, 64]]),
                        bass.AP(ar_t, sblk, [[PSTR, 128], [1, 64]]))
            inc(last, "cV")
        kvt2_v = cnt["cV"]
        kvt2_tp = kstg_tp

        @block.vector
        def _(v):
            wt(v, "sTPK", kstg_tp)
            v.drain()
            src = bass.AP(ar_t, KSTG + 64, [[PSTR, 128], [80, 32]])
            inc(v.tensor_copy(ksumc[:, :], src), "cV")
        ksumc_v = cnt["cV"]

        qf_used = {0: 0, 1: 0}
        attn_ev = 0
        expq_s = {6: 0, 7: 0}
        eq_nc_used = {6: 0, 7: 0}
        uargs_done = {6: 0, 7: 0}
        q2q_used = 0
        tp_last_qf = [0]
        for p in range(8):
            @block.vector
            def _(v):
                wt(v, "cV", qk_done_v)
                wt(v, "cc2", agk_done)
                wt(v, "cT", q2q_used)
                wt(v, "cT", max(kf_used.values()))
                v.drain()
                inc(v.tensor_tensor(b16v(A_Q2, TOK), b16v(A_Q + p * TOK, TOK),
                                    b16v(A_Q + p * TOK, TOK), ALU.mult), "cV")
            q2_s = cnt["cV"]

            qfe_s = 0
            qf_tp_prev = tp_last_qf[0]
            for ts in range(NT):
                ub = 6 + (ts % 2)
                ncol = 8 + 2 * (ub - 6)
                rcol = 2 * (ub - 6)

                @block.tensor
                def _(tt):
                    wt(tt, "cV", q2_s)
                    wt(tt, "cV", uargs_done[ub])
                    lhs = b16v(A_Q + p * TOK + ts * 128, 128)
                    tt.ldweights(lhs)
                    tt.matmul(bank(ub, 256), lhs, pm2[:, p * FF:(p + 1) * FF],
                              start=True, stop=False)
                    lhs2 = b16v(A_Q2 + ts * 128, 128)
                    tt.ldweights(lhs2)
                    inc(tt.matmul(bank(ub, 2, off=256), lhs2, nhalf[:, :],
                                  start=False, stop=True), "cT")
                q2q_used = cnt["cT"]
                uq_t = cnt["cT"]

                earg = EARG + (ub - 6) * 1024

                @block.vector
                def _(v):
                    wt(v, "cT", uq_t)
                    wt(v, "cS", eq_nc_used[ub])
                    v.drain()
                    v.tensor_reduce(rmts[:, rcol:rcol + 2],
                                    bank(ub, 256).rearrange(
                                        "p (h f) -> p h f", f=128),
                                    AX.X, ALU.max)
                    v.drain()
                    v.tensor_tensor(negc[:, ncol:ncol + 2],
                                    bank(ub, 2, off=256),
                                    rmts[:, rcol:rcol + 2],
                                    ALU.subtract)
                    v.drain()
                    for hh in range(2):
                        ncc = negc[:, ncol + hh:ncol + hh + 1]
                        v.tensor_scalar(f32v(earg + hh * 512, 128),
                                        bank(ub, 128, off=hh * 128), ncc,
                                        None, ALU.add)
                        v.tensor_scalar(f32v(earg + hh * 512 + 256, 128),
                                        bank(ub, 128, off=hh * 128), ncc,
                                        -1.0, ALU.subtract, ALU.mult)
                    inc(v.drain(), "cV")
                ncq_v = cnt["cV"]

                @block.scalar
                def _(sc):
                    wt(sc, "cV", ncq_v)
                    wt(sc, "sTPQ", qf_tp_prev)
                    dst3 = bass.AP(ar_t, A_QF + ts * 128,
                                   [[PSTR, 128], [TOK, 4], [1, 128]])
                    inc(sc.activation(dst3, f32v(earg, 512), AF.Exp), "cS")
                expq_s[ub] = cnt["cS"]
                eq_nc_used[ub] = cnt["cS"]
                uargs_done[ub] = ncq_v
            qfe_s = cnt["cS"]

            tp = {}
            for hh in range(2):
                @block.scalar
                def _(sc):
                    wt(sc, "cS", qfe_s)
                    wt(sc, "cT", qf_used[hh])
                    for fc in range(2):
                        src_t = bass.AP(ar_t, A_QF + hh * (2 * TOK) + fc * TOK,
                                        [[PSTR, 128], [128, NT], [1, 128]])
                        dst_t = bass.AP(ar_t,
                                        A_QFT + hh * (2 * TOK) + fc * TOK,
                                        [[PSTR, 128], [128, NT], [1, 128]])
                        inc(sc.dma_start_transpose(dst_t, src_t), "sTPQ")
                tp[hh] = cnt["sTPQ"]
            tp_last_qf[0] = cnt["sTPQ"]

            for tb in range(TB):
                @block.tensor
                def _(tt):
                    wt(tt, "sTPQ", tp[1])
                    wt(tt, "sTPK", kvt2_tp)
                    wt(tt, "cV", kvt2_v)
                    wt(tt, "cV", ksumc_v)
                    wt(tt, "cV", attn_ev)
                    for mt in range(4):
                        hh, fc = mt // 2, mt % 2
                        lhs = b16v(A_KVT2 + p * 512 + mt * 128, 128)
                        tt.ldweights(lhs)
                        tt.matmul(bank(tb), lhs,
                                  b16v(A_QFT + hh * (2 * TOK) + fc * TOK
                                       + tb * 512, 512),
                                  start=(mt == 0), stop=(mt == 3))
                    for hh in range(2):
                        zrows = psA[hh * 64:(hh + 1) * 64,
                                    4 * 512:5 * 512]
                        for fc in range(2):
                            kc = bass.AP(ksumc.ap().tensor,
                                         (2 * p + hh) * 2 + fc,
                                         [[32, 128], [0, 64]])
                            tt.matmul(zrows, kc,
                                      b16v(A_QFT + hh * (2 * TOK) + fc * TOK
                                           + tb * 512, 512),
                                      start=(fc == 0), stop=False)
                        inc(tt.matmul(zrows, onesf[0:1, 0:64],
                                      epsrow[0:1, :], start=False, stop=True),
                            "cT")
                nz_t = cnt["cT"]

                @block.vector
                def _(v):
                    wt(v, "cT", nz_t)
                    v.drain()
                    v.reciprocal(f32v(RZF, 512), bank(4, 512))
                    v.drain()
                    inc(v.tensor_tensor(
                        b16v(A_ATT + p * TOK + tb * 512, 512),
                        bank(tb), f32v(RZF, 512), ALU.mult), "cV")
                attn_ev = cnt["cV"]
            qf_used[0] = cnt["cT"]
            qf_used[1] = cnt["cT"]
        attn_done_v = cnt["cV"]

        if dbg_attn:
            @block.vector
            def _(v):
                wt(v, "cV", attn_done_v)
                v.drain()
                inc(v.tensor_copy(f32v(A_LN2R, 1024),
                                  psA[:, 4 * 512:6 * 512]), "cV")
            dbgz_v = cnt["cV"]

            @block.sync
            def _(s):
                wt(s, "cV", dbgz_v)
                inc(s.dma_start(out=dbga, in_=b16v(A_ATT, 8 * TOK)), "sDBG")
                inc(s.dma_start(out=dbgr, in_=b16v(A_RZT, 512)), "sDBG")
                inc(s.dma_start(out=dbgq, in_=b16v(A_QFT, 4 * TOK)), "sDBG")
                inc(s.dma_start(out=dbgk, in_=b16v(A_KVT2, 4096)), "sDBG")
                inc(s.dma_start(out=dbgc, in_=negc[:, 0:4]), "sDBG")
                inc(s.dma_start(out=dbgs, in_=ksumc[:, :]), "sDBG")
                inc(s.dma_start(out=dbgz, in_=f32v(A_LN2R, 1024)), "sDBG")
            ctx["dbg_done"] = cnt["sDBG"]

        @block.gpsimd
        def _(g):
            wt(g, "cT", cnt["cT"])
            inc(g.dma_start(
                out=b16v(A_WV, 8192).rearrange("p (k c) -> p k c", k=8),
                in_=wout_d.rearrange("(k p) c -> p k c", p=128)), "sWV")
        wout_ld = cnt["sWV"]

        woutev = {0: 0, 1: 0}
        xs2p = {0: 0, 1: 0}
        for ts in range(NT):
            @block.sync
            def _(s):
                wt(s, "cV", attn_done_v)
                wt(s, "cV", woutev[ts % 2])
                wt(s, "sXS2%d" % (ts % 2), xs2p[ts % 2])
                inc(s.dma_start(out=f32v(A_XT + (ts % 2) * 2048, 1024),
                                in_=xsp1.ap()[ts * 128:(ts + 1) * 128, :]),
                    "sX1")
            x1ld = cnt["sX1"]

            @block.tensor
            def _(tt):
                wt(tt, "sWV", wout_ld)
                wt(tt, "cV", attn_done_v)
                wt(tt, "cV", woutev[ts % 2])
                for k in range(8):
                    lhs = b16v(A_ATT + k * TOK + ts * 128, 128)
                    tt.ldweights(lhs)
                    for vb in range(2):
                        last = tt.matmul(bank((ts % 2) * 2 + vb), lhs,
                                         b16v(A_WV + k * 1024 + vb * 512, 512),
                                         start=(k == 0), stop=(k == 7))
                inc(last, "cT")
            wo_t = cnt["cT"]

            @block.vector
            def _(v):
                wt(v, "cT", wo_t)
                wt(v, "sX1", x1ld)
                xb = f32v(A_XT + (ts % 2) * 2048, 1024)
                for vb in range(2):
                    last = v.tensor_tensor(xb[:, vb * 512:(vb + 1) * 512],
                                           xb[:, vb * 512:(vb + 1) * 512],
                                           bank((ts % 2) * 2 + vb), ALU.add)
                inc(last, "cV")
            x2_v = cnt["cV"]

            @block.sync
            def _(s):
                wt(s, "cV", x2_v)
                inc(s.dma_start(out=xsp2.ap()[ts * 128:(ts + 1) * 128, :],
                                in_=f32v(A_XT + (ts % 2) * 2048, 1024)),
                    "sXS2%d" % (ts % 2))
            woutev[ts % 2] = x2_v
            xs2p[ts % 2] = cnt["sXS2%d" % (ts % 2)]
        x2_done = (cnt["sXS20"], cnt["sXS21"])

        @block.vector
        def _(v):
            wt(v, "sXS20", x2_done[0])
            wt(v, "sXS21", x2_done[1])
            if dbg_attn:
                wt(v, "sDBG", ctx["dbg_done"])
            inc(v.drain(), "cV")
        st8["resid_v"] = cnt["cV"]
        st8["resid_t"] = 0

        emit_ffn(3)

        _epilogue(block, wt, cnt)
    mybir.codegen_inst_isa_subclasses(nc)
    return nc


_NC_CACHE = {}


def kernel(L0_attn_x, L0_attn_W_qkv, L0_attn_W_out, L0_attn_proj,
           L0_ffn_w_gate, L0_ffn_w_up, L0_ffn_w_down):
    x = np.ascontiguousarray(np.asarray(L0_attn_x, dtype=np.float32))
    B, N, Dm = x.shape
    TOK = B * N // NCORE
    NT = TOK // 128
    if "nc" not in _NC_CACHE or _NC_CACHE.get("NT") != NT:
        _NC_CACHE["nc"] = build_nc(NT)
        _NC_CACHE["NT"] = NT
    nc = _NC_CACHE["nc"]

    wqkv = np.ascontiguousarray(np.asarray(L0_attn_W_qkv, np.float32))
    wout = np.ascontiguousarray(np.asarray(L0_attn_W_out, np.float32))
    proj = np.asarray(L0_attn_proj, np.float32)
    pmt = np.ascontiguousarray(proj.transpose(2, 0, 1).reshape(HD, NH * F2))
    wg = np.ascontiguousarray(np.asarray(L0_ffn_w_gate, np.float32))
    wu = np.ascontiguousarray(np.asarray(L0_ffn_w_up, np.float32))
    wdn = np.ascontiguousarray(np.asarray(L0_ffn_w_down, np.float32))

    xs = x.reshape(B * N, Dm)
    in_maps = []
    for c in range(NCORE):
        in_maps.append({
            "x": np.ascontiguousarray(xs[c * TOK:(c + 1) * TOK]),
            "wqkv": wqkv,
            "wout": wout,
            "pmt": pmt,
            "wg": np.ascontiguousarray(wg[c * ISH:(c + 1) * ISH]),
            "wu": np.ascontiguousarray(wu[c * ISH:(c + 1) * ISH]),
            "wd": np.ascontiguousarray(wdn[c * 128:(c + 1) * 128]),
        })
    res = run_bass_kernel_spmd(nc, in_maps, core_ids=list(range(NCORE)))
    out = np.concatenate([res.results[c]["out"] for c in range(NCORE)], axis=0)
    return out.reshape(B, N, Dm)


# version 16 (final)
# speedup vs baseline: 1.0190x; 1.0064x over previous
import numpy as np
import ml_dtypes

import concourse.bass as bass
import concourse.mybir as mybir
from concourse import dve_ops
from concourse.dve_spec import Spec, Src0, Src1, One, C2, lower
from concourse.dve_table_gen import dve_ver_for
from concourse.bass_utils import run_bass_kernel_spmd

F32 = mybir.dt.float32
BF16 = mybir.dt.bfloat16
ALU = mybir.AluOpType
AX = mybir.AxisListType
AF = mybir.ActivationFunctionType


def _register_qdq_ops():
    have = {o.name: o for o in dve_ops.OPS}
    if "QDQ_A" in have:
        return have["QDQ_A"], have["QDQ_T"]
    from concourse.dve_spec import C0, C1
    u = Src0
    a_body = ((u > One) + (u > C0)) + ((u > C1) + (u > C2))
    c5, c6, c7 = u > C0, u > C1, u > C2
    i1 = c5 + c6
    i2 = c7 + c7
    j = i1 + i2
    t_body = Src1 + (j + j)

    def a_ref(in0, in1, s0, s1, imm2):
        x = in0
        one = np.asarray(1.0, x.dtype)
        zero = np.asarray(0.0, x.dtype)
        r = np.where(x > np.asarray(1.0, x.dtype), one, zero)
        for K in (s0, s1, imm2):
            r = r + np.where(x > np.asarray(K, x.dtype), one, zero)
        return r.astype(x.dtype)

    def t_ref(in0, in1, s0, s1, imm2):
        uu, A = in0, in1
        one = np.asarray(1.0, uu.dtype)
        zero = np.asarray(0.0, uu.dtype)
        j = np.where(uu > np.asarray(s0, uu.dtype), one, zero)
        j = j + np.where(uu > np.asarray(s1, uu.dtype), one, zero)
        j = j + np.asarray(2.0, uu.dtype) * np.where(
            uu > np.asarray(imm2, uu.dtype), one, zero)
        return (A + j + j).astype(uu.dtype)

    ver = dve_ver_for("TRN2")
    spec_a, spec_t = Spec(body=a_body), Spec(body=t_body)
    op_a = dve_ops.DveOp("QDQ_A", spec_a, subdim=False,
                         uops_sha={ver: lower(spec_a, ver=ver, name="QDQ_A").sha(ver)},
                         reference=a_ref)
    op_t = dve_ops.DveOp("QDQ_T", spec_t, subdim=False,
                         uops_sha={ver: lower(spec_t, ver=ver, name="QDQ_T").sha(ver)},
                         reference=t_ref)
    names = [o.name for o in dve_ops.OPS]
    for op, old in ((op_a, "DVEHASH32L_SEED"), (op_t, "DVEHASH32L_BODY")):
        idx = names.index(old)
        dve_ops.OPS[idx] = op
        row = dve_ops._SUB_OPCODE_FOR_NAME.pop(old)
        dve_ops._SUB_OPCODE_FOR_NAME[op.name] = row
        dve_ops.CUSTOM_DVE_SPECS.pop(old, None)
        dve_ops.CUSTOM_DVE_SPECS[op.name] = op.spec
        dve_ops.CUSTOM_DVE_REFS.pop(old, None)
        dve_ops.CUSTOM_DVE_REFS[op.name] = op.ref
    return op_a, op_t


QDQ_A, QDQ_T = _register_qdq_ops()

D = 1024
I = 2816
NH = 16
HD = 64
F2 = 128
FF = 256
NCORE = 8
NGRP = 4
ISH = I // 8
WPART = 8 * ISH
WRANK = 3 * WPART
EPS_LN = 1e-5
EPS_Z = 256e-6
ARENA = 101120

X0 = 0
LNR = 32768
XQR = 36864
QDQ_SC = 38912
QSM = 47872
SILU = 52096
HR = 53504
HQR = 56320
XQT = 61952
HQT = 66048
WBUF = 77312
WDB = 88576
OUTR = 90624
WSH = 92672
HR2 = 92672
HQ23 = 95488
WSHQ = 66048
WSHT = 77312
PML = 88576
A_LN2B = 32768
A_K = 0
A_RANK = 0
A_KVACC = 4224
A_ATT = 0
A_LN2T = 16384
A_QF = 16384
A_QFT = 24576
A_Q = 32768
A_RZT = 65792
RZF = 65792
EARG = 66816
A_XT = 37888
A_V65 = 49152
A_KVGB = 49152
A_KVT2 = 53248
A_Q2 = 57344
A_KVP = 65792
A_WV = 74240
A_LN2R = 82432
A_KF = 86528
A_WQK = 87040


def _epilogue(block, wt, cnt):
    @block.sync
    def _(s):
        for ch in ("sLDX", "sLDPM", "sLDSH", "sSP", "sAGW", "sAGK",
                   "sRK", "sST", "sX1", "sXS20", "sXS21", "sOUT",
                   "sWG0", "sWG1", "sWD0", "sWD1", "sWD2", "sWD3",
                   "sOR0", "sOR1", "sX2P0", "sX2P1"):
            wt(s, ch, cnt[ch])

    @block.scalar
    def _(sc):
        for ch in ("sTPX0", "sTPX1", "sTPH0", "sTPH1", "sTPH2", "sTPH3",
                   "sTPL", "sTPQ",
                   "sTPK", "sTPW"):
            wt(sc, ch, cnt[ch])
        wt(sc, "cS", cnt["cS"])

    @block.gpsimd
    def _(g):
        for ch in ("sQK0", "sQK1", "sWV", "cc1", "cc2"):
            wt(g, ch, cnt[ch])
        wt(g, "cG", cnt["cG"])

    @block.tensor
    def _(tt):
        wt(tt, "cT", cnt["cT"])

    @block.vector
    def _(v):
        wt(v, "cV", cnt["cV"])


def build_nc(NT=16, stop_after=None, dbg_spill=False, dbg_attn=False):
    TOK = NT * 128
    NTQ = max(NT // 4, 1)
    NQ = NT // NTQ
    TB = TOK // 512
    nc = bass.Bass("TRN2", target_bir_lowering=False)

    x_d = nc.dram_tensor("x", [TOK, D], F32, kind="ExternalInput").ap()
    wqkv_d = nc.dram_tensor("wqkv", [D, 3 * D], F32, kind="ExternalInput").ap()
    wout_d = nc.dram_tensor("wout", [D, D], F32, kind="ExternalInput").ap()
    pmt_d = nc.dram_tensor("pmt", [HD, NH * F2], F32, kind="ExternalInput").ap()
    wg_d = nc.dram_tensor("wg", [ISH, D], F32, kind="ExternalInput").ap()
    wu_d = nc.dram_tensor("wu", [ISH, D], F32, kind="ExternalInput").ap()
    wd_d = nc.dram_tensor("wd", [128, I], F32, kind="ExternalInput").ap()
    out_d = nc.dram_tensor("out", [TOK, D], F32, kind="ExternalOutput").ap()

    agw_in = nc.dram_tensor("agw_in", [128, WRANK], BF16)
    agw_out = nc.dram_tensor("agw_out", [NCORE, 128, WRANK], BF16)
    wdc = nc.dram_tensor("wdc", [2 * 22 * 128 * 512], BF16)
    KVE = 65 * NH * 264
    agk_in = nc.dram_tensor("agk_in", [1, KVE + 32], BF16)
    agk_out = nc.dram_tensor("agk_out", [NGRP, KVE + 32], BF16)
    spk = dict(kind="ExternalOutput") if dbg_spill else {}
    xsp1 = nc.dram_tensor("xsp1", [TOK, D], F32, **spk)
    xsp2 = nc.dram_tensor("xsp2", [TOK, D], F32, **spk)
    if dbg_attn:
        dbga = nc.dram_tensor("dbga", [128, 8 * TOK], BF16,
                              kind="ExternalOutput").ap()
        dbgr = nc.dram_tensor("dbgr", [128, 512], BF16,
                              kind="ExternalOutput").ap()
        dbgq = nc.dram_tensor("dbgq", [128, 4 * TOK], BF16,
                              kind="ExternalOutput").ap()
        dbgk = nc.dram_tensor("dbgk", [128, 4096], BF16,
                              kind="ExternalOutput").ap()
        dbgc = nc.dram_tensor("dbgc", [128, 4], F32,
                              kind="ExternalOutput").ap()
        dbgs = nc.dram_tensor("dbgs", [128, 32], BF16,
                              kind="ExternalOutput").ap()
        dbgz = nc.dram_tensor("dbgz", [128, 1024], F32,
                              kind="ExternalOutput").ap()

    ctx = {}
    from contextlib import ExitStack
    es = ExitStack()
    with es:
        ARN = es.enter_context(nc.sbuf_tensor("arena", [128, ARENA], BF16))
        pm2 = es.enter_context(nc.sbuf_tensor("pm2", [128, 8 * FF], BF16))
        identb = es.enter_context(nc.sbuf_tensor("identb", [128, 128], BF16))
        identf = es.enter_context(nc.sbuf_tensor("identf", [128, 128], F32))
        onesf = es.enter_context(nc.sbuf_tensor("onesf", [1, 130], F32))
        epsrow = es.enter_context(nc.sbuf_tensor("epsrow", [1, 512], F32))
        nhalf = es.enter_context(nc.sbuf_tensor("nhalf", [128, 2], BF16))
        negstab = es.enter_context(nc.sbuf_tensor("negstab", [1, 16], F32))
        stabrow = es.enter_context(nc.sbuf_tensor("stabrow", [1, 96], F32))
        rowmaxes = es.enter_context(nc.sbuf_tensor("rowmaxes", [128, 16], F32))
        rmts = es.enter_context(nc.sbuf_tensor("rmts", [128, 32], F32))
        stat = es.enter_context(nc.sbuf_tensor("stat", [128, 20], F32))
        negc = es.enter_context(nc.sbuf_tensor("negc", [128, 12], F32))
        iot = es.enter_context(nc.sbuf_tensor("iot", [128, 128], F32))
        piot = es.enter_context(nc.sbuf_tensor("piot", [128, 1], F32))
        ksumc = es.enter_context(nc.sbuf_tensor("ksumc", [128, 32], BF16))
        scr = es.enter_context(nc.sbuf_tensor("scr", [65, 64], BF16))
        stabhl = es.enter_context(nc.sbuf_tensor("stabhl", [1, 160], BF16))
        stabf = es.enter_context(nc.sbuf_tensor("stabf", [1, 16], F32))
        cpool = es.enter_context(nc.sbuf_tensor("cpool", [128, 4], F32))
        psA = es.enter_context(nc.psum_tensor("ps", [128, 4096], F32))
        SEM_NAMES = ("cT", "cV", "cS", "cG",
                     "sLDX", "sLDPM", "sLDSH", "sWG0", "sWG1",
                     "sWD0", "sWD1", "sWD2", "sWD3",
                     "sOR0", "sOR1", "sOUT", "sSP", "sAGW", "sAGK",
                     "sRK", "sST", "sX1", "sXS2", "sX2P0", "sX2P1",
                     "sTPX0", "sTPX1", "sTPH0", "sTPH1", "sTPH2", "sTPH3",
                     "sTPL", "sTPQ",
                     "sTPK", "sTPW", "sQK0", "sQK1", "sWV", "cc1", "cc2",
                     "sXS20", "sXS21", "sDBG", "sWDC")
        sems = {n: es.enter_context(nc.semaphore(n)) for n in SEM_NAMES}
        block = es.enter_context(nc.Block())
        ar_t = ARN.ap().tensor
        PSTR = ARN.ap().ap[0][0]

        def b16v(col, width, rows=128):
            assert col + width <= ARENA, (col, width)
            return bass.AP(ar_t, col, [[PSTR, rows], [1, width]])

        def f32v(col, width, rows=128):
            assert col % 2 == 0 and col + 2 * width <= ARENA, (col, width)
            return bass.AP(ar_t, col, [[PSTR, rows], [1, 2 * width]]).bitcast(F32)

        def f32bc(col, rows, B):
            ap = f32v(col, B, rows)
            return bass.AP(ap.tensor, ap.offset,
                           [[ap.ap[0][0], rows], [1, B], [0, 16]])

        def bank(i, cols=512, rows=128, off=0):
            return psA[0:rows, 512 * i + off: 512 * i + off + cols]

        cnt = {k: 0 for k in sems}
        _ENG = {"cT", "cV", "cS", "cG"}

        def inc(inst, ch):
            step = 1 if (ch in _ENG or ch.startswith("cc")) else 16
            inst.then_inc(sems[ch], step)
            cnt[ch] += step
            return cnt[ch]

        def wt(eng, ch, val):
            if val > 0:
                eng.wait_ge(sems[ch], val)

        @block.sync
        def _(s):
            xdst = f32v(X0, NT * 1024).rearrange("p (t d) -> p t d", d=1024)
            inc(s.dma_start(out=xdst, in_=x_d[:, :].rearrange(
                "(t p) d -> p t d", p=128)), "sLDX")
        ld_x = cnt["sLDX"]

        @block.sync
        def _(s):
            inc(s.dma_start(out=f32v(PML, 2048, rows=HD), in_=pmt_d[:, :]),
                "sLDPM")
        ld_pm = cnt["sLDPM"]

        nc.const_aps.register(F32, EPS_LN, cpool[:, 0:1])
        nc.const_aps.register(F32, 0.0, cpool[:, 1:2])
        nc.const_aps.register(F32, 1.0, cpool[:, 2:3])

        @block.vector
        def _(v):
            v.memset(cpool[:, 0:1], EPS_LN)
            v.memset(cpool[:, 1:2], 0.0)
            v.memset(cpool[:, 2:3], 1.0)
            v.memset(cpool[:, 3:4], -0.0)
            v.memset(onesf[:, :], 1.0)
            v.memset(epsrow[:, :], EPS_Z)
            v.memset(nhalf[:, :], 0.0)
            v.memset(pm2[:, :], 0.0)
            v.memset(rowmaxes[:, :], -3.0e38)
            v.drain()
            v.memset(nhalf[0:64, 0:1], -0.5)
            inc(v.memset(nhalf[64:128, 1:2], -0.5), "cV")
        consts_v = cnt["cV"]

        @block.gpsimd
        def _(g):
            g.iota(iot[:, :], [[1, 128]], channel_multiplier=0,
                   allow_small_or_imprecise_dtypes=True)
            inc(g.iota(piot[:, :], [[0, 1]], channel_multiplier=1,
                       allow_small_or_imprecise_dtypes=True), "cG")
        iota_g = cnt["cG"]

        @block.vector
        def _(v):
            wt(v, "cG", iota_g)
            v.tensor_single_scalar(identf[:, :], iot[:, :], piot[:, 0:1],
                                   ALU.is_equal)
            v.drain()
            inc(v.tensor_copy(identb[:, :], identf[:, :]), "cV")
        ident_v = cnt["cV"]

        @block.scalar
        def _(sc):
            wt(sc, "sLDPM", ld_pm)
            wt(sc, "cV", consts_v)
            pml = f32v(PML, 2048, rows=HD)
            for p in range(8):
                sc.mul(pm2[0:HD, p * FF: p * FF + F2],
                       pml[:, (2 * p) * F2:(2 * p) * F2 + F2], 0.125)
                last = sc.mul(pm2[64:64 + HD, p * FF + F2:(p + 1) * FF],
                              pml[:, (2 * p + 1) * F2:(2 * p + 1) * F2 + F2],
                              0.125)
            inc(last, "cS")
        pm2_s = cnt["cS"]

        qdq_prev = {0: 0, 1: 0}

        def emit_qdq(src, dst, W, rows=128, rot=0, waits=()):
            base = QDQ_SC + rot * 4480
            um = f32v(base, W, rows)
            am = f32v(base + 2048, W, rows)
            mcol, icol = base + 4096, base + 4224
            B = W // 16
            mm_ = f32v(mcol, B, rows)
            im_ = f32v(icol, B, rows)
            sm = f32v(QSM + rot * 2048, W, rows)
            src3 = src.rearrange("p (b c) -> p b c", c=16)
            prev_g = qdq_prev[rot]
            I32 = mybir.dt.int32

            @block.vector
            def _(v):
                wt(v, "cG", prev_g)
                for ch, val in waits:
                    wt(v, ch, val)
                v.drain()
                v.tensor_single_scalar(sm.bitcast(I32), src.bitcast(I32),
                                       cpool[0:rows, 3:4].bitcast(I32),
                                       ALU.bitwise_and)
                v.tensor_reduce(mm_, src3, AX.X, ALU.max,
                                apply_absolute_value=True)
                v.drain()
                v.tensor_scalar(mm_, mm_, 1.0 / 12.0, 0.5e-12, ALU.mult,
                                ALU.max)
                v.drain()
                v.reciprocal(im_, mm_)
                v.drain()
                inc(v.tensor_scalar_mul(im_, im_, 2.0), "cV")
            minv_v = cnt["cV"]

            @block.gpsimd
            def _(g):
                wt(g, "cV", minv_v)
                g.tensor_tensor(
                    um.rearrange("p (b c) -> p b c", c=16), src3,
                    f32bc(icol, rows, B), ALU.mult)
                g.drain()
                inc(g.tensor_tensor(um, um, um, ALU.mult), "cG")
            ns_g = cnt["cG"]

            @block.vector
            def _(v):
                wt(v, "cG", ns_g)
                v.drain()
                v._custom_dve(QDQ_A, out=am, in0=um, s0=9.0, s1=25.0,
                              imm2=49.0)
                v.drain()
                with nc.allow_low_precision(reason="T ladder ints exact"):
                    v._custom_dve(QDQ_T, out=dst, in0=um, in1=am, s0=100.0,
                                  s1=196.0, imm2=400.0)
                v.drain()
                inc(v.tensor_tensor(sm.bitcast(I32), sm.bitcast(I32),
                                    f32bc(mcol, rows, B).bitcast(I32),
                                    ALU.bitwise_xor), "cV")
            lad_v = cnt["cV"]

            @block.gpsimd
            def _(g):
                wt(g, "cV", lad_v)
                g.drain()
                inc(g.tensor_tensor(dst, dst, sm, ALU.mult), "cG")
            qdq_prev[rot] = cnt["cG"]
            return cnt["cG"]

        tp_prev = 0

        def prep_gateup(w_hbm, sec_off, tp_prev):
            ld = []

            @block.sync
            def _(s):
                for j, (r0, P) in enumerate(((0, 128), (128, 128), (256, 96))):
                    inc(s.dma_start(
                        out=f32v(WSH + j * 2048, 1024, rows=P),
                        in_=w_hbm[r0:r0 + P, :]), "sLDSH")
            ld_total = cnt["sLDSH"]
            for j, (r0, P) in enumerate(((0, 128), (128, 128), (256, 96))):
                src = f32v(WSH + j * 2048, 1024, rows=P)
                q = emit_qdq(
                    src, bass.AP(ar_t, WSHQ, [[PSTR, P], [1, 1024]]),
                    1024, rows=P, rot=j % 2,
                    waits=(("sLDSH", ld_total), ("sTPW", tp_prev)))

                @block.scalar
                def _(sc):
                    wt(sc, "cG", q)
                    wt(sc, "sAGW", cnt["sAGW"])
                    src_t = bass.AP(ar_t, WSHQ, [[PSTR, P], [128, 8], [1, 128]])
                    dst_t = bass.AP(ar_t, WSHT + j * 128,
                                    [[PSTR, 128], [ISH, 8], [1, P]])
                    inc(sc.dma_start_transpose(dst_t, src_t),
                        "sTPW")
                tp_prev = cnt["sTPW"]

            @block.sync
            def _(s):
                wt(s, "sTPW", tp_prev)
                inc(s.dma_start(out=agw_in.ap()[:, sec_off:sec_off + WPART],
                                in_=b16v(WSHT, WPART)), "sAGW")
            return tp_prev

        tp_prev = prep_gateup(wg_d, 0, tp_prev)
        tp_prev = prep_gateup(wu_d, WPART, tp_prev)

        @block.sync
        def _(s):
            wt(s, "cG", qdq_prev[0])
            wt(s, "cG", qdq_prev[1])
            inc(s.dma_start(out=f32v(WSH, 2816), in_=wd_d[:, :]), "sLDSH")
        ld_wd = cnt["sLDSH"]
        for ci, (cpos, cw) in enumerate(((0, 1024), (1024, 1024),
                                         (2048, 768))):
            q = emit_qdq(f32v(WSH, 2816)[:, cpos:cpos + cw],
                         b16v(WSHQ, 2816)[:, cpos:cpos + cw], cw,
                         rot=ci % 2,
                         waits=(("sLDSH", ld_wd), ("sTPW", tp_prev)))

        @block.scalar
        def _(sc):
            wt(sc, "cG", q)
            wt(sc, "sAGW", cnt["sAGW"])
            src_t = bass.AP(ar_t, WSHQ, [[PSTR, 128], [128, 22], [1, 128]])
            dst_t = bass.AP(ar_t, WSHT, [[PSTR, 128], [128, 22], [1, 128]])
            inc(sc.dma_start_transpose(dst_t, src_t), "sTPW")
        wd_tp = cnt["sTPW"]

        @block.sync
        def _(s):
            wt(s, "sTPW", wd_tp)
            inc(s.dma_start(out=agw_in.ap()[:, 2 * WPART:3 * WPART],
                            in_=b16v(WSHT, WPART)), "sAGW")
        st_d = cnt["sAGW"]

        @block.gpsimd
        def _(g):
            wt(g, "sAGW", st_d)
            inc(g.collective_compute(
                "AllGather", ALU.bypass, replica_groups=[list(range(NCORE))],
                ins=[agw_in.ap().opt()], outs=[agw_out.ap().opt()]), "cc1")
        agw_done = cnt["cc1"]

        @block.sync
        def _(s):
            wt(s, "cc1", agw_done)
            ag_t = agw_out.ap().tensor
            for r in range(8):
                hb2, rr = r // 4, r % 4
                src = bass.AP(ag_t, r * 128 * WRANK + 2 * WPART,
                              [[WRANK, 128], [128, 22], [1, 128]])
                dst = bass.AP(wdc.ap().tensor, hb2 * 22 * 128 * 512 + rr * 128,
                              [[512, 128], [128 * 512, 22], [1, 128]])
                inc(s.dma_start(out=dst, in_=src), "sWDC")
        wdc_done = cnt["sWDC"]

        st8 = {"resid_t": 0, "resid_v": 0}

        def emit_ffn(kind):
            x_dram = (kind == 3)
            wslot_t = {0: 0, 1: 0}
            wdslot_t = {}
            bankev = {0: 0, 1: 0}
            silu_used = {0: 0, 1: 0}
            hused = {}
            lnst_prev = 0
            down_prev_t = 0
            outdma = {0: 0, 1: 0}
            hq_tp = {0: 0, 1: 0, 2: 0, 3: 0}
            def prep_tiles(tiles, xq_tp):
                nonlocal lnst_prev
                for jt, t in enumerate(tiles):
                    rot = jt % 2
                    if x_dram:
                        xt = f32v(X0 + rot * 2048, 1024)
                        xch = "sX2P%d" % rot

                        @block.sync
                        def _(s):
                            wt(s, "cV", lnst_prev)
                            wt(s, "cV", st8["resid_v"])
                            inc(s.dma_start(
                                out=xt,
                                in_=xsp2.ap()[t * 128:(t + 1) * 128, :]), xch)
                        x_wait = (xch, cnt[xch])
                    else:
                        xt = f32v(X0 + t * 2048, 1024)
                        x_wait = ("sLDX", ld_x)
                    lnr = f32v(LNR + rot * 2048, 1024)

                    @block.vector
                    def _(v):
                        wt(v, x_wait[0], x_wait[1])
                        wt(v, "cT", st8["resid_t"])
                        v.drain()
                        v.bn_stats(stat[:, 0:6], xt[:, 0:512])
                        v.bn_stats(stat[:, 6:12], xt[:, 512:1024])
                        v.drain()
                        inc(v.bn_aggr(stat[:, 12:14], stat[:, 0:12]), "cV")
                    lnst_prev = cnt["cV"]

                    @block.scalar
                    def _(sc):
                        wt(sc, "cV", lnst_prev)
                        inc(sc.activation(stat[:, 14:15], stat[:, 13:14],
                                          AF.Sqrt, bias=EPS_LN), "cS")
                    sq_s = cnt["cS"]

                    @block.vector
                    def _(v):
                        wt(v, "cS", sq_s)
                        v.drain()
                        v.reciprocal(stat[:, 15:16], stat[:, 14:15])
                        v.drain()
                        inc(v.tensor_scalar(lnr, xt, stat[:, 12:13],
                                            stat[:, 15:16], ALU.subtract,
                                            ALU.mult), "cV")
                    ln_v = cnt["cV"]

                    xqt = b16v(XQR + rot * 1024, 1024)
                    tpch = "sTPX%d" % rot
                    q = emit_qdq(lnr, xqt, 1024, rot=rot,
                                 waits=(("cV", ln_v), (tpch, xq_tp[rot])))

                    @block.scalar
                    def _(sc):
                        wt(sc, "cG", q)
                        wt(sc, "cT", wslot_t[0])
                        wt(sc, "cT", wslot_t[1])
                        src_t = bass.AP(ar_t, XQR + rot * 1024,
                                        [[PSTR, 128], [128, 8], [1, 128]])
                        dst_t = bass.AP(ar_t, XQT + jt * 128,
                                        [[PSTR, 128], [NTQ * 128, 8], [1, 128]])
                        inc(sc.dma_start_transpose(dst_t, src_t),
                            tpch)
                    xq_tp[rot] = cnt[tpch]

            xq_tp_q = {0: {0: 0, 1: 0}}
            prep_tiles(list(range(NTQ)), xq_tp_q[0])
            for tq in range(NQ):
                tiles = list(range(tq * NTQ, (tq + 1) * NTQ))
                xq_tp = xq_tp_q[tq]

                for r in range(8):
                    sl = r % 2

                    wch = "sWG%d" % sl

                    @block.sync
                    def _(s):
                        wt(s, "cc1", agw_done)
                        wt(s, "cT", wslot_t[sl])
                        inc(s.dma_start(out=b16v(WBUF + sl * 2816, 2816),
                                        in_=agw_out.ap()[r, :, 0:WPART]), wch)
                        inc(s.dma_start(
                            out=b16v(WBUF + (2 + sl) * 2816, 2816),
                            in_=agw_out.ap()[r, :, WPART:2 * WPART]), wch)
                    wld = cnt[wch]
                    for jt, t in enumerate(tiles):
                        bA = (jt % 2) * 2
                        bB = bA + 1
                        hcol = (HR if jt < 2 else HR2) + (jt % 2) * 1408

                        @block.tensor
                        def _(tt):
                            wt(tt, wch, wld)
                            wt(tt, "sTPX0", xq_tp[0])
                            wt(tt, "sTPX1", xq_tp[1])
                            wt(tt, "cV", bankev[jt % 2])
                            for k in range(8):
                                lhs = b16v(XQT + k * NTQ * 128 + jt * 128, 128)
                                tt.ldweights(lhs)
                                tt.matmul(bank(bA, 352), lhs,
                                          b16v(WBUF + sl * 2816 + k * 352, 352),
                                          start=(k == 0), stop=(k == 7))
                                last = tt.matmul(
                                    bank(bB, 352), lhs,
                                    b16v(WBUF + (2 + sl) * 2816 + k * 352, 352),
                                    start=(k == 0), stop=(k == 7))
                            inc(last, "cT")
                        wslot_t[sl] = cnt["cT"]
                        mm_t = cnt["cT"]

                        @block.scalar
                        def _(sc):
                            wt(sc, "cT", mm_t)
                            wt(sc, "cV", silu_used[sl])
                            inc(sc.activation(f32v(SILU + sl * 704, 352),
                                              bank(bA, 352), AF.Silu), "cS")
                        si_s = cnt["cS"]

                        @block.vector
                        def _(v):
                            wt(v, "cS", si_s)
                            wt(v, "cG", hused.get((jt, r % 2), 0))
                            v.drain()
                            hb = f32v(hcol, 704)
                            inc(v.tensor_tensor(
                                hb[:, (r % 2) * 352:(r % 2) * 352 + 352],
                                bank(bB, 352), f32v(SILU + sl * 704, 352),
                                ALU.mult), "cV")
                        bankev[jt % 2] = cnt["cV"]
                        silu_used[sl] = cnt["cV"]
                        hev = cnt["cV"]

                        if r % 2 == 1:
                            hb = f32v(hcol, 704)
                            hqcol = ((HQR if jt < 2 else HQ23)
                                     + (jt % 2) * 2816)
                            hq = b16v(hqcol, 2816)
                            q = emit_qdq(hb, hq[:, (r - 1) * 352:(r + 1) * 352],
                                         704, rot=jt % 2,
                                         waits=(("cV", hev),
                                                ("sTPH%d" % jt,
                                                 hq_tp[jt])))
                            hused[(jt, 0)] = cnt["cG"]
                            hused[(jt, 1)] = cnt["cG"]
                            if r == 7:
                                @block.scalar
                                def _(sc):
                                    wt(sc, "cG", q)
                                    wt(sc, "cT", down_prev_t)
                                    src_t = bass.AP(
                                        ar_t, hqcol,
                                        [[PSTR, 128], [128, 22], [1, 128]])
                                    dst_t = bass.AP(
                                        ar_t, HQT + jt * 128,
                                        [[PSTR, 128], [NTQ * 128, 22], [1, 128]])
                                    inc(sc.dma_start_transpose(
                                        dst_t, src_t),
                                        "sTPH%d" % jt)
                                hq_tp[jt] = cnt["sTPH%d" % jt]

                if tq + 1 < NQ:
                    xq_tp_q[tq + 1] = {0: xq_tp[0], 1: xq_tp[1]}
                    prep_tiles(list(range((tq + 1) * NTQ, (tq + 2) * NTQ)),
                               xq_tp_q[tq + 1])

                for hb2 in range(2):
                    for ic in range(22):
                        wslot = ic % 4
                        wdch = "sWD%d" % wslot

                        @block.sync
                        def _(s):
                            wt(s, "sWDC", wdc_done)
                            wt(s, "cT", wdslot_t.get(wslot, 0))
                            src = bass.AP(
                                wdc.ap().tensor,
                                (hb2 * 22 + ic) * 128 * 512,
                                [[512, 128], [1, 512]])
                            inc(s.dma_start(out=b16v(WDB + wslot * 512, 512),
                                            in_=src), wdch)
                        wdld = cnt[wdch]

                        @block.tensor
                        def _(tt):
                            wt(tt, wdch, wdld)
                            for _jt in range(NTQ):
                                wt(tt, "sTPH%d" % _jt, hq_tp[_jt])
                            if ic == 0:
                                wt(tt, "cV", st8["resid_v"])
                            for jt in range(NTQ):
                                lhs = b16v(HQT + ic * NTQ * 128 + jt * 128, 128)
                                tt.ldweights(lhs)
                                last = tt.matmul(
                                    bank(4 + jt), lhs,
                                    b16v(WDB + wslot * 512, 512),
                                    start=(ic == 0), stop=(ic == 21))
                            inc(last, "cT")
                        wdslot_t[wslot] = cnt["cT"]
                    dn_t = cnt["cT"]
                    down_prev_t = dn_t

                    for jt, t in enumerate(tiles):
                        if kind == 1:
                            @block.vector
                            def _(v):
                                wt(v, "cT", dn_t)
                                xsl = f32v(X0 + t * 2048 + hb2 * 1024, 512)
                                inc(v.scalar_tensor_tensor(
                                    xsl, bank(4 + jt), 0.5, xsl, ALU.mult,
                                    ALU.add), "cV")
                            st8["resid_v"] = cnt["cV"]
                        else:
                            orch = "sOR%d" % (t % 2)

                            @block.sync
                            def _(s):
                                wt(s, "cV", st8["resid_v"])
                                wt(s, "sOUT", cnt["sOUT"])
                                inc(s.dma_start(
                                    out=f32v(OUTR + (t % 2) * 1024, 512),
                                    in_=xsp2.ap()[t * 128:(t + 1) * 128,
                                                  hb2 * 512:(hb2 + 1) * 512]),
                                    orch)
                            x2l = cnt[orch]

                            @block.vector
                            def _(v):
                                wt(v, "cT", dn_t)
                                wt(v, orch, x2l)
                                orow = f32v(OUTR + (t % 2) * 1024, 512)
                                inc(v.scalar_tensor_tensor(
                                    orow, bank(4 + jt), 0.5, orow, ALU.mult,
                                    ALU.add), "cV")
                            st8["resid_v"] = cnt["cV"]
                            ev_v = cnt["cV"]

                            @block.sync
                            def _(s):
                                wt(s, "cV", ev_v)
                                inc(s.dma_start(
                                    out=out_d[t * 128:(t + 1) * 128,
                                              hb2 * 512:(hb2 + 1) * 512],
                                    in_=f32v(OUTR + (t % 2) * 1024, 512)),
                                    "sOUT")
                            outdma[t % 2] = cnt["sOUT"]
                    if kind == 1:
                        st8["resid_t"] = 0
            return

        emit_ffn(1)
        ffn1_v = cnt["cV"]

        if stop_after == "ffn1":
            _epilogue(block, wt, cnt)
            return nc

        lnst_prev = 0
        for t in range(NT):
            xt = f32v(X0 + t * 2048, 1024)

            @block.sync
            def _(s):
                wt(s, "cV", ffn1_v)
                inc(s.dma_start(out=xsp1.ap()[t * 128:(t + 1) * 128, :],
                                in_=xt), "sSP")

            @block.vector
            def _(v):
                wt(v, "cV", ffn1_v)
                v.drain()
                v.bn_stats(stat[:, 0:6], xt[:, 0:512])
                v.bn_stats(stat[:, 6:12], xt[:, 512:1024])
                v.drain()
                inc(v.bn_aggr(stat[:, 12:14], stat[:, 0:12]), "cV")
            lnst_prev = cnt["cV"]

            @block.scalar
            def _(sc):
                wt(sc, "cV", lnst_prev)
                inc(sc.activation(stat[:, 14:15], stat[:, 13:14], AF.Sqrt,
                                  bias=EPS_LN), "cS")
            sq_s = cnt["cS"]

            @block.vector
            def _(v):
                wt(v, "cS", sq_s)
                v.drain()
                v.reciprocal(stat[:, 15:16], stat[:, 14:15])
                v.drain()
                with nc.allow_low_precision(reason="ln2 bf16 rows (as twin)"):
                    inc(v.tensor_scalar(b16v(A_LN2B + t * 1024, 1024), xt,
                                        stat[:, 12:13], stat[:, 15:16],
                                        ALU.subtract, ALU.mult), "cV")
            ln_v = cnt["cV"]
        ln2b_v = cnt["cV"]

        @block.scalar
        def _(sc):
            wt(sc, "cV", ln2b_v)
            wt(sc, "sSP", cnt["sSP"])
            for t in range(NT):
                src_t = bass.AP(ar_t, A_LN2B + t * 1024,
                                [[PSTR, 128], [128, 8], [1, 128]])
                dst_t = bass.AP(ar_t, A_LN2T + t * 128,
                                [[PSTR, 128], [TOK, 8], [1, 128]])
                inc(sc.dma_start_transpose(dst_t, src_t), "sTPL")
        ln2T_tp = cnt["sTPL"]

        @block.gpsimd
        def _(g):
            wt(g, "cV", ln2b_v)
            inc(g.dma_start(
                out=b16v(A_WV, 8192).rearrange("p (k c) -> p k c", k=8),
                in_=wqkv_d[:, 2048:3072].rearrange("(k p) c -> p k c", p=128)),
                "sWV")
        wv_ld = cnt["sWV"]

        qkev = {0: 0, 1: 0}
        wqk_used = {0: 0, 1: 0}
        for ck in range(16):
            sl = ck % 2

            qch = "sQK%d" % sl

            @block.gpsimd
            def _(g):
                wt(g, "cT", wqk_used[sl])
                inc(g.dma_start(
                    out=b16v(A_WQK + sl * 1024, 1024).rearrange(
                        "p (k c) -> p k c", k=8),
                    in_=wqkv_d[:, ck * 128:(ck + 1) * 128].rearrange(
                        "(k p) c -> p k c", p=128)), qch)
            wqk_ld = cnt[qch]

            @block.tensor
            def _(tt):
                wt(tt, qch, wqk_ld)
                wt(tt, "sTPL", ln2T_tp)
                wt(tt, "cV", qkev[sl])
                grp = sl * 4
                for k in range(8):
                    lhs = b16v(A_WQK + sl * 1024 + k * 128, 128)
                    tt.ldweights(lhs)
                    for tb in range(TB):
                        last = tt.matmul(bank(grp + tb), lhs,
                                         b16v(A_LN2T + k * TOK + tb * 512, 512),
                                         start=(k == 0), stop=(k == 7))
                inc(last, "cT")
            wqk_used[sl] = cnt["cT"]
            qk_t = cnt["cT"]

            @block.vector
            def _(v):
                wt(v, "cT", qk_t)
                grp = sl * 4
                base = (A_Q + ck * TOK) if ck < 8 else (A_K + (ck - 8) * TOK)
                for tb in range(TB):
                    last = v.tensor_copy(b16v(base + tb * 512, 512),
                                         bank(grp + tb))
                inc(last, "cV")
            qkev[sl] = cnt["cV"]
        qk_done_v = cnt["cV"]

        @block.vector
        def _(v):
            wt(v, "cV", ln2b_v)
            v.memset(b16v(A_KVP, NH * 264, rows=65), 0.0)
            v.drain()
            v.memset(b16v(A_V65, NT * 1040), 0.0)
            v.drain()
            ones_ap = bass.AP(ar_t, A_V65 + 1024,
                              [[PSTR, 128], [1040, NT], [65, 16]])
            inc(v.memset(ones_ap, 1.0), "cV")
        v65i = cnt["cV"]

        vev = dict(qkev)
        for ts in range(NT):
            @block.tensor
            def _(tt):
                wt(tt, "sWV", wv_ld)
                wt(tt, "sTPL", ln2T_tp)
                wt(tt, "cV", vev[ts % 2])
                for k in range(8):
                    lhs = b16v(A_LN2T + k * TOK + ts * 128, 128)
                    tt.ldweights(lhs)
                    for vb in range(2):
                        last = tt.matmul(bank((ts % 2) * 2 + vb), lhs,
                                         b16v(A_WV + k * 1024 + vb * 512, 512),
                                         start=(k == 0), stop=(k == 7))
                inc(last, "cT")
            v_t = cnt["cT"]

            @block.vector
            def _(v):
                wt(v, "cT", v_t)
                wt(v, "cV", v65i)
                for vb in range(2):
                    dst = bass.AP(ar_t, A_V65 + ts * 1040 + vb * 8 * 65,
                                  [[PSTR, 128], [65, 8], [1, 64]])
                    last = v.tensor_copy(dst, bank((ts % 2) * 2 + vb).rearrange(
                        "p (h d) -> p h d", d=64))
                inc(last, "cV")
            vev[ts % 2] = cnt["cV"]
        v_done_v = cnt["cV"]

        upar = {6: v_done_v, 7: v_done_v}
        for p in range(8):
            for ts in range(NT):
                ub = 6 + (ts % 2)

                @block.tensor
                def _(tt):
                    wt(tt, "cS", pm2_s)
                    wt(tt, "cV", qk_done_v)
                    wt(tt, "cV", upar[ub])
                    lhs = b16v(A_K + p * TOK + ts * 128, 128)
                    tt.ldweights(lhs)
                    inc(tt.matmul(bank(ub, 256), lhs,
                                  pm2[:, p * FF:(p + 1) * FF],
                                  start=True, stop=True), "cT")
                u_t = cnt["cT"]

                @block.vector
                def _(v):
                    wt(v, "cT", u_t)
                    v.drain()
                    inc(v.tensor_reduce(
                        rmts[:, 2 * ts:2 * ts + 2], bank(ub, 256).rearrange(
                            "p (h f) -> p h f", f=128), AX.X, ALU.max), "cV")
                upar[ub] = cnt["cV"]

            @block.vector
            def _(v):
                v.drain()
                src = bass.AP(rmts.ap().tensor, 0, [[32, 128], [1, 2], [2, NT]])
                inc(v.tensor_reduce(rowmaxes[:, 2 * p:2 * p + 2], src,
                                    AX.X, ALU.max), "cV")
        rmax_v = cnt["cV"]

        @block.tensor
        def _(tt):
            wt(tt, "cV", rmax_v)
            wt(tt, "cV", ident_v)
            inc(tt.transpose(psA[0:16, 3072:3200], rowmaxes[:, :],
                             identf[:, :]), "cT")
        rtp_t = cnt["cT"]

        @block.vector
        def _(v):
            wt(v, "cT", rtp_t)
            v.tensor_reduce(stat[0:16, 16:17], psA[0:16, 3072:3200], AX.X,
                            ALU.max)
            v.drain()
            inc(v.tensor_scalar_mul(stat[0:16, 17:18], stat[0:16, 16:17],
                                    -1.0), "cV")
        nstc_v = cnt["cV"]

        @block.tensor
        def _(tt):
            wt(tt, "cV", nstc_v)
            inc(tt.transpose(psA[0:1, 3072:3088], stat[0:16, 17:18],
                             identf[0:16, 0:16]), "cT")
        nst_t = cnt["cT"]

        @block.vector
        def _(v):
            wt(v, "cT", nst_t)
            v.tensor_copy(negstab[0:1, 0:16], psA[0:1, 3072:3088])
            v.drain()
            with nc.allow_low_precision(reason="hi/lo bf16 pair is exact"):
                v.tensor_copy(stabhl[0:1, 0:16], negstab[0:1, 0:16])
            v.drain()
            v.tensor_copy(stabf[0:1, :], stabhl[0:1, 0:16])
            v.drain()
            with nc.allow_low_precision(reason="hi/lo bf16 pair is exact"):
                inc(v.tensor_tensor(stabhl[0:1, 16:32], negstab[0:1, 0:16],
                                    stabf[0:1, :], ALU.subtract), "cV")
        negstab_v = cnt["cV"]

        KF4 = A_LN2T
        AGB = 24576
        args_done = {0: 0, 1: 0, 2: 0, 3: 0}
        argb_used = {6: 0, 7: 0}
        kf_used = {0: 0, 1: 0, 2: 0, 3: 0}
        kv_read_v = 0
        q2_used = 0
        for p in range(8):
            @block.vector
            def _(v):
                wt(v, "cV", v_done_v)
                wt(v, "cT", q2_used)
                v.drain()
                inc(v.tensor_tensor(b16v(A_Q2, TOK), b16v(A_K + p * TOK, TOK),
                                    b16v(A_K + p * TOK, TOK), ALU.mult), "cV")
            k2_s = cnt["cV"]

            st_c2 = {}

            def emit_u2(ts, p=p, k2_s=k2_s):
                ub = ts % 4
                bb = 6 + (ts % 2)

                @block.tensor
                def _(tt):
                    wt(tt, "cV", k2_s)
                    wt(tt, "cV", args_done[ub])
                    wt(tt, "cV", negstab_v)
                    lhs = b16v(A_K + p * TOK + ts * 128, 128)
                    tt.ldweights(lhs)
                    tt.matmul(bank(ub, 256), lhs, pm2[:, p * FF:(p + 1) * FF],
                              start=True, stop=False)
                    lhs2 = b16v(A_Q2 + ts * 128, 128)
                    tt.ldweights(lhs2)
                    tt.matmul(bank(ub, 2, off=256), lhs2, nhalf[:, :],
                              start=False, stop=False)
                    inc(tt.matmul(bank(ub, 2, off=256), onesf[0:1, 0:128],
                                  negstab[0:1, 2 * p:2 * p + 2],
                                  start=False, stop=True), "cT")
                st_c2[("u2", ts)] = cnt["cT"]

                agb = AGB + (bb - 6) * 1024

                @block.vector
                def _(v):
                    wt(v, "cT", st_c2[("u2", ts)])
                    wt(v, "cS", argb_used[bb])
                    v.drain()
                    for hh in range(2):
                        nd = bank(ub, 1, off=256 + hh)
                        v.tensor_scalar(f32v(agb + hh * 512, 128),
                                        bank(ub, 128, off=hh * 128), nd,
                                        None, ALU.add)
                        v.tensor_scalar(f32v(agb + hh * 512 + 256, 128),
                                        bank(ub, 128, off=hh * 128), nd,
                                        -1.0, ALU.subtract, ALU.mult)
                    inc(v.drain(), "cV")
                args_done[ub] = cnt["cV"]
                st_c2[("args", ts)] = cnt["cV"]

                @block.scalar
                def _(sc):
                    wt(sc, "cV", st_c2[("args", ts)])
                    wt(sc, "cT", kf_used[ub])
                    inc(sc.activation(b16v(KF4 + ub * 512, 512),
                                      f32v(agb, 512), AF.Exp), "cS")
                argb_used[bb] = cnt["cS"]
                st_c2[("kf", ts)] = cnt["cS"]

            def emit_kv(ts, p=p):
                ub = ts % 4

                @block.tensor
                def _(tt):
                    wt(tt, "cS", st_c2[("kf", ts)])
                    if ts == 0:
                        wt(tt, "cV", kv_read_v)
                    for hh in range(2):
                        lhs = bass.AP(ar_t,
                                      A_V65 + ts * 1040 + (2 * p + hh) * 65,
                                      [[PSTR, 128], [1, 65]])
                        tt.ldweights(lhs)
                        last = tt.matmul(bank(4 + hh, 256, rows=65), lhs,
                                         b16v(KF4 + ub * 512 + hh * 256, 256),
                                         start=(ts == 0), stop=(ts == NT - 1))
                    inc(last, "cT")
                kf_used[ub] = cnt["cT"]

            for ts in range(NT):
                emit_u2(ts)
                if ts >= 3:
                    emit_kv(ts - 3)
            for ts in range(NT - 3, NT):
                emit_kv(ts)
            q2_used = cnt["cT"]
            kv_t = cnt["cT"]

            @block.vector
            def _(v):
                wt(v, "cT", kv_t)
                for hh in range(2):
                    dst = bass.AP(ar_t, A_KVP + (2 * p + hh) * 264,
                                  [[PSTR, 65], [1, 256]])
                    last = v.tensor_copy(dst, bank(4 + hh, 256, rows=65))
                inc(last, "cV")
            kv_read_v = cnt["cV"]
        kvp_v = cnt["cV"]

        @block.sync
        def _(s):
            wt(s, "cV", kvp_v)
            inc(s.dma_start(out=agk_in.ap()[0:1, 0:KVE].rearrange(
                "a (p w) -> (a p) w", p=65),
                in_=b16v(A_KVP, NH * 264, rows=65)), "sAGK")
            inc(s.dma_start(
                out=agk_in.ap()[0:1, KVE:KVE + 32],
                in_=stabhl[0:1, 0:32]), "sAGK")
        agks = cnt["sAGK"]

        @block.gpsimd
        def _(g):
            wt(g, "sAGK", agks)
            inc(g.collective_compute(
                "AllGather", ALU.bypass,
                replica_groups=[[0, 1, 2, 3], [4, 5, 6, 7]],
                ins=[agk_in.ap().opt()], outs=[agk_out.ap().opt()]), "cc2")
        agk_done = cnt["cc2"]

        @block.sync
        def _(s):
            wt(s, "cc2", agk_done)
            src = bass.AP(agk_out.ap().tensor, agk_out.ap().offset + KVE,
                          [[KVE + 32, NGRP], [1, 32]])
            inc(s.dma_start(out=stabhl[0:1, 32:160], in_=src), "sST")
        stld = cnt["sST"]

        @block.vector
        def _(v):
            wt(v, "sST", stld)
            v.drain()
            hi = bass.AP(stabhl.ap().tensor, 32, [[160, 1], [32, 4], [1, 16]])
            lo = bass.AP(stabhl.ap().tensor, 48, [[160, 1], [32, 4], [1, 16]])
            v.tensor_tensor(stabrow[0:1, 0:64].rearrange(
                "a (r h) -> a r h", h=16), hi, lo, ALU.add)
            v.drain()
            vsrc = bass.AP(stabrow.ap().tensor, 0, [[96, 1], [1, 16], [16, 4]])
            v.tensor_reduce(stabrow[0:1, 64:80], vsrc, AX.X, ALU.min)
            v.drain()
            gb = bass.AP(stabrow.ap().tensor, 64, [[96, 1], [0, 4], [1, 16]])
            dv = stabrow[0:1, 0:64]
            inc(v.tensor_tensor(dv, gb, dv, ALU.subtract), "cV")
        stsc_v = cnt["cV"]

        @block.scalar
        def _(sc):
            wt(sc, "cV", stsc_v)
            inc(sc.activation(stabrow[0:1, 0:64], stabrow[0:1, 0:64], AF.Exp),
                "cS")
        scexp_s = cnt["cS"]

        @block.tensor
        def _(tt):
            wt(tt, "cS", scexp_s)
            inc(tt.matmul(psA[0:65, 3072:3136], onesf[0:1, 0:65],
                          stabrow[0:1, 0:64], start=True, stop=True), "cT")
        scr_t = cnt["cT"]

        @block.vector
        def _(v):
            wt(v, "cT", scr_t)
            inc(v.tensor_copy(scr[:, :], psA[0:65, 3072:3136]), "cV")
        scr_v = cnt["cV"]

        cmb_v = 0
        for r in range(NGRP):
            @block.sync
            def _(s):
                wt(s, "cc2", agk_done)
                wt(s, "cV", cmb_v)
                inc(s.dma_start(
                    out=b16v(A_RANK, NH * 264, rows=65),
                    in_=agk_out.ap()[r, 0:KVE].rearrange("(p w) -> p w", p=65)),
                    "sRK")
            rkld = cnt["sRK"]

            @block.vector
            def _(v):
                wt(v, "sRK", rkld)
                wt(v, "cV", scr_v)
                v.drain()
                scb = bass.AP(scr.ap().tensor, r * 16,
                              [[64, 65], [1, 16], [0, 264]])
                rk3 = bass.AP(ar_t, A_RANK, [[PSTR, 65], [264, 16], [1, 264]])
                v.tensor_tensor(rk3, rk3, scb, ALU.mult)
                v.drain()
                if r == 0:
                    last = v.tensor_copy(b16v(A_KVACC, NH * 264, rows=65),
                                         b16v(A_RANK, NH * 264, rows=65))
                else:
                    last = v.tensor_tensor(b16v(A_KVACC, NH * 264, rows=65),
                                           b16v(A_KVACC, NH * 264, rows=65),
                                           b16v(A_RANK, NH * 264, rows=65),
                                           ALU.add)
                inc(last, "cV")
            cmb_v = cnt["cV"]

        @block.vector
        def _(v):
            wt(v, "cV", cmb_v)
            wt(v, "cV", v_done_v)
            v.drain()
            v.memset(bass.AP(ar_t, A_KVGB, [[PSTR, 80], [1, 4096]]), 0.0)
            v.drain()
            dst = bass.AP(ar_t, A_KVGB, [[PSTR, 65], [256, 16], [1, 256]])
            src = bass.AP(ar_t, A_KVACC, [[PSTR, 65], [264, 16], [1, 256]])
            inc(v.tensor_copy(dst, src), "cV")
        kvgb_v = cnt["cV"]

        @block.vector
        def _(v):
            v.memset(b16v(A_KVT2, 4096), 0.0)
            inc(v.drain(), "cV")
        kvt2_z = cnt["cV"]

        KSTG = A_QF
        @block.scalar
        def _(sc):
            wt(sc, "cV", kvgb_v)
            wt(sc, "cT", kf_used[0])
            wt(sc, "cT", kf_used[1])
            src_t = bass.AP(ar_t, A_KVGB,
                            [[PSTR, 80], [128, 32], [1, 128]])
            dst_t = bass.AP(ar_t, KSTG,
                            [[PSTR, 128], [80, 32], [1, 80]])
            inc(sc.dma_start_transpose(dst_t, src_t), "sTPK")
        kstg_tp = cnt["sTPK"]

        @block.vector
        def _(v):
            wt(v, "sTPK", kstg_tp)
            wt(v, "cV", kvt2_z)
            v.drain()
            for h in range(16):
                pair, hh = h // 2, h % 2
                for fc in range(2):
                    sblk = KSTG + (h * 2 + fc) * 80
                    dcol = A_KVT2 + pair * 512 + (hh * 2 + fc) * 128 + hh * 64
                    last = v.tensor_copy(
                        bass.AP(ar_t, dcol, [[PSTR, 128], [1, 64]]),
                        bass.AP(ar_t, sblk, [[PSTR, 128], [1, 64]]))
            inc(last, "cV")
        kvt2_v = cnt["cV"]
        kvt2_tp = kstg_tp

        @block.vector
        def _(v):
            wt(v, "sTPK", kstg_tp)
            v.drain()
            src = bass.AP(ar_t, KSTG + 64, [[PSTR, 128], [80, 32]])
            inc(v.tensor_copy(ksumc[:, :], src), "cV")
        ksumc_v = cnt["cV"]

        qf_used = {0: 0, 1: 0}
        attn_ev = 0
        expq_s = {6: 0, 7: 0}
        eq_nc_used = {6: 0, 7: 0}
        uargs_done = {6: 0, 7: 0}
        q2q_used = 0
        tp_last_qf = [0]
        for p in range(8):
            @block.vector
            def _(v):
                wt(v, "cV", qk_done_v)
                wt(v, "cc2", agk_done)
                wt(v, "cT", q2q_used)
                wt(v, "cT", max(kf_used.values()))
                v.drain()
                inc(v.tensor_tensor(b16v(A_Q2, TOK), b16v(A_Q + p * TOK, TOK),
                                    b16v(A_Q + p * TOK, TOK), ALU.mult), "cV")
            q2_s = cnt["cV"]

            qfe_s = 0
            qf_tp_prev = tp_last_qf[0]
            for ts in range(NT):
                ub = 6 + (ts % 2)
                ncol = 8 + 2 * (ub - 6)
                rcol = 2 * (ub - 6)

                @block.tensor
                def _(tt):
                    wt(tt, "cV", q2_s)
                    wt(tt, "cV", uargs_done[ub])
                    lhs = b16v(A_Q + p * TOK + ts * 128, 128)
                    tt.ldweights(lhs)
                    tt.matmul(bank(ub, 256), lhs, pm2[:, p * FF:(p + 1) * FF],
                              start=True, stop=False)
                    lhs2 = b16v(A_Q2 + ts * 128, 128)
                    tt.ldweights(lhs2)
                    inc(tt.matmul(bank(ub, 2, off=256), lhs2, nhalf[:, :],
                                  start=False, stop=True), "cT")
                q2q_used = cnt["cT"]
                uq_t = cnt["cT"]

                earg = EARG + (ub - 6) * 1024

                @block.vector
                def _(v):
                    wt(v, "cT", uq_t)
                    wt(v, "cS", eq_nc_used[ub])
                    v.drain()
                    v.tensor_reduce(rmts[:, rcol:rcol + 2],
                                    bank(ub, 256).rearrange(
                                        "p (h f) -> p h f", f=128),
                                    AX.X, ALU.max)
                    v.drain()
                    v.tensor_tensor(negc[:, ncol:ncol + 2],
                                    bank(ub, 2, off=256),
                                    rmts[:, rcol:rcol + 2],
                                    ALU.subtract)
                    v.drain()
                    for hh in range(2):
                        ncc = negc[:, ncol + hh:ncol + hh + 1]
                        v.tensor_scalar(f32v(earg + hh * 512, 128),
                                        bank(ub, 128, off=hh * 128), ncc,
                                        None, ALU.add)
                        v.tensor_scalar(f32v(earg + hh * 512 + 256, 128),
                                        bank(ub, 128, off=hh * 128), ncc,
                                        -1.0, ALU.subtract, ALU.mult)
                    inc(v.drain(), "cV")
                ncq_v = cnt["cV"]

                @block.scalar
                def _(sc):
                    wt(sc, "cV", ncq_v)
                    wt(sc, "sTPQ", qf_tp_prev)
                    dst3 = bass.AP(ar_t, A_QF + ts * 128,
                                   [[PSTR, 128], [TOK, 4], [1, 128]])
                    inc(sc.activation(dst3, f32v(earg, 512), AF.Exp), "cS")
                expq_s[ub] = cnt["cS"]
                eq_nc_used[ub] = cnt["cS"]
                uargs_done[ub] = ncq_v
            qfe_s = cnt["cS"]

            tp = {}
            for hh in range(2):
                @block.scalar
                def _(sc):
                    wt(sc, "cS", qfe_s)
                    wt(sc, "cT", qf_used[hh])
                    for fc in range(2):
                        src_t = bass.AP(ar_t, A_QF + hh * (2 * TOK) + fc * TOK,
                                        [[PSTR, 128], [128, NT], [1, 128]])
                        dst_t = bass.AP(ar_t,
                                        A_QFT + hh * (2 * TOK) + fc * TOK,
                                        [[PSTR, 128], [128, NT], [1, 128]])
                        inc(sc.dma_start_transpose(dst_t, src_t), "sTPQ")
                tp[hh] = cnt["sTPQ"]
            tp_last_qf[0] = cnt["sTPQ"]

            for tb in range(TB):
                @block.tensor
                def _(tt):
                    wt(tt, "sTPQ", tp[1])
                    wt(tt, "sTPK", kvt2_tp)
                    wt(tt, "cV", kvt2_v)
                    wt(tt, "cV", ksumc_v)
                    wt(tt, "cV", attn_ev)
                    for mt in range(4):
                        hh, fc = mt // 2, mt % 2
                        lhs = b16v(A_KVT2 + p * 512 + mt * 128, 128)
                        tt.ldweights(lhs)
                        tt.matmul(bank(tb), lhs,
                                  b16v(A_QFT + hh * (2 * TOK) + fc * TOK
                                       + tb * 512, 512),
                                  start=(mt == 0), stop=(mt == 3))
                    for hh in range(2):
                        zrows = psA[hh * 64:(hh + 1) * 64,
                                    4 * 512:5 * 512]
                        for fc in range(2):
                            kc = bass.AP(ksumc.ap().tensor,
                                         (2 * p + hh) * 2 + fc,
                                         [[32, 128], [0, 64]])
                            tt.matmul(zrows, kc,
                                      b16v(A_QFT + hh * (2 * TOK) + fc * TOK
                                           + tb * 512, 512),
                                      start=(fc == 0), stop=False)
                        inc(tt.matmul(zrows, onesf[0:1, 0:64],
                                      epsrow[0:1, :], start=False, stop=True),
                            "cT")
                nz_t = cnt["cT"]

                @block.vector
                def _(v):
                    wt(v, "cT", nz_t)
                    v.drain()
                    v.reciprocal(f32v(RZF, 512), bank(4, 512))
                    v.drain()
                    inc(v.tensor_tensor(
                        b16v(A_ATT + p * TOK + tb * 512, 512),
                        bank(tb), f32v(RZF, 512), ALU.mult), "cV")
                attn_ev = cnt["cV"]
            qf_used[0] = cnt["cT"]
            qf_used[1] = cnt["cT"]
        attn_done_v = cnt["cV"]

        if dbg_attn:
            @block.vector
            def _(v):
                wt(v, "cV", attn_done_v)
                v.drain()
                inc(v.tensor_copy(f32v(A_LN2R, 1024),
                                  psA[:, 4 * 512:6 * 512]), "cV")
            dbgz_v = cnt["cV"]

            @block.sync
            def _(s):
                wt(s, "cV", dbgz_v)
                inc(s.dma_start(out=dbga, in_=b16v(A_ATT, 8 * TOK)), "sDBG")
                inc(s.dma_start(out=dbgr, in_=b16v(A_RZT, 512)), "sDBG")
                inc(s.dma_start(out=dbgq, in_=b16v(A_QFT, 4 * TOK)), "sDBG")
                inc(s.dma_start(out=dbgk, in_=b16v(A_KVT2, 4096)), "sDBG")
                inc(s.dma_start(out=dbgc, in_=negc[:, 0:4]), "sDBG")
                inc(s.dma_start(out=dbgs, in_=ksumc[:, :]), "sDBG")
                inc(s.dma_start(out=dbgz, in_=f32v(A_LN2R, 1024)), "sDBG")
            ctx["dbg_done"] = cnt["sDBG"]

        @block.gpsimd
        def _(g):
            wt(g, "cT", cnt["cT"])
            inc(g.dma_start(
                out=b16v(A_WV, 8192).rearrange("p (k c) -> p k c", k=8),
                in_=wout_d.rearrange("(k p) c -> p k c", p=128)), "sWV")
        wout_ld = cnt["sWV"]

        woutev = {0: 0, 1: 0}
        xs2p = {0: 0, 1: 0}
        for ts in range(NT):
            @block.sync
            def _(s):
                wt(s, "cV", attn_done_v)
                wt(s, "cV", woutev[ts % 2])
                wt(s, "sXS2%d" % (ts % 2), xs2p[ts % 2])
                inc(s.dma_start(out=f32v(A_XT + (ts % 2) * 2048, 1024),
                                in_=xsp1.ap()[ts * 128:(ts + 1) * 128, :]),
                    "sX1")
            x1ld = cnt["sX1"]

            @block.tensor
            def _(tt):
                wt(tt, "sWV", wout_ld)
                wt(tt, "cV", attn_done_v)
                wt(tt, "cV", woutev[ts % 2])
                for k in range(8):
                    lhs = b16v(A_ATT + k * TOK + ts * 128, 128)
                    tt.ldweights(lhs)
                    for vb in range(2):
                        last = tt.matmul(bank((ts % 2) * 2 + vb), lhs,
                                         b16v(A_WV + k * 1024 + vb * 512, 512),
                                         start=(k == 0), stop=(k == 7))
                inc(last, "cT")
            wo_t = cnt["cT"]

            @block.vector
            def _(v):
                wt(v, "cT", wo_t)
                wt(v, "sX1", x1ld)
                xb = f32v(A_XT + (ts % 2) * 2048, 1024)
                for vb in range(2):
                    last = v.tensor_tensor(xb[:, vb * 512:(vb + 1) * 512],
                                           xb[:, vb * 512:(vb + 1) * 512],
                                           bank((ts % 2) * 2 + vb), ALU.add)
                inc(last, "cV")
            x2_v = cnt["cV"]

            @block.sync
            def _(s):
                wt(s, "cV", x2_v)
                inc(s.dma_start(out=xsp2.ap()[ts * 128:(ts + 1) * 128, :],
                                in_=f32v(A_XT + (ts % 2) * 2048, 1024)),
                    "sXS2%d" % (ts % 2))
            woutev[ts % 2] = x2_v
            xs2p[ts % 2] = cnt["sXS2%d" % (ts % 2)]
        x2_done = (cnt["sXS20"], cnt["sXS21"])

        @block.vector
        def _(v):
            wt(v, "sXS20", x2_done[0])
            wt(v, "sXS21", x2_done[1])
            if dbg_attn:
                wt(v, "sDBG", ctx["dbg_done"])
            inc(v.drain(), "cV")
        st8["resid_v"] = cnt["cV"]
        st8["resid_t"] = 0

        emit_ffn(3)

        _epilogue(block, wt, cnt)
    mybir.codegen_inst_isa_subclasses(nc)
    return nc


_NC_CACHE = {}


def kernel(L0_attn_x, L0_attn_W_qkv, L0_attn_W_out, L0_attn_proj,
           L0_ffn_w_gate, L0_ffn_w_up, L0_ffn_w_down):
    x = np.ascontiguousarray(np.asarray(L0_attn_x, dtype=np.float32))
    B, N, Dm = x.shape
    TOK = B * N // NCORE
    NT = TOK // 128
    if "nc" not in _NC_CACHE or _NC_CACHE.get("NT") != NT:
        _NC_CACHE["nc"] = build_nc(NT)
        _NC_CACHE["NT"] = NT
    nc = _NC_CACHE["nc"]

    wqkv = np.ascontiguousarray(np.asarray(L0_attn_W_qkv, np.float32))
    wout = np.ascontiguousarray(np.asarray(L0_attn_W_out, np.float32))
    proj = np.asarray(L0_attn_proj, np.float32)
    pmt = np.ascontiguousarray(proj.transpose(2, 0, 1).reshape(HD, NH * F2))
    wg = np.ascontiguousarray(np.asarray(L0_ffn_w_gate, np.float32))
    wu = np.ascontiguousarray(np.asarray(L0_ffn_w_up, np.float32))
    wdn = np.ascontiguousarray(np.asarray(L0_ffn_w_down, np.float32))

    xs = x.reshape(B * N, Dm)
    in_maps = []
    for c in range(NCORE):
        in_maps.append({
            "x": np.ascontiguousarray(xs[c * TOK:(c + 1) * TOK]),
            "wqkv": wqkv,
            "wout": wout,
            "pmt": pmt,
            "wg": np.ascontiguousarray(wg[c * ISH:(c + 1) * ISH]),
            "wu": np.ascontiguousarray(wu[c * ISH:(c + 1) * ISH]),
            "wd": np.ascontiguousarray(wdn[c * 128:(c + 1) * 128]),
        })
    res = run_bass_kernel_spmd(nc, in_maps, core_ids=list(range(NCORE)))
    out = np.concatenate([res.results[c]["out"] for c in range(NCORE)], axis=0)
    return out.reshape(B, N, Dm)
